# version 04
# baseline (speedup 1.0000x reference)
import numpy as np
import concourse.bass as bass
import concourse.mybir as mybir
from concourse.bass_utils import run_bass_kernel_spmd

F32 = mybir.dt.float32
F32R = mybir.dt.float32r
BF16 = mybir.dt.bfloat16
I32 = mybir.dt.int32
ALU = mybir.AluOpType
AF = mybir.ActivationFunctionType

H = 4096
DIN = 8192
NST = 16
KCV = 4
R = 256
T = 2048
NB = 8
NKH = H // 128
NDC = DIN // 128
NMP = 2 * DIN // 128
NMO = H // 128

_DEBUG = False
_TAPS = False
_SUB2 = 3
ENGINES = ("sync", "vector", "scalar", "tensor", "gpsimd")


class Ctx:

    def __init__(self, nc):
        self.nc = nc
        self.recs = {e: [] for e in ENGINES}
        self.sems = {}
        self.cnt = {}

    def sem(self, name):
        if name not in self.sems:
            self.sems[name] = self.nc.semaphore(name).__enter__()
            self.cnt[name] = 0
        return self.sems[name]

    def add(self, eng, emit, waits=(), incs=()):
        ws = []
        for p in waits:
            if p is None:
                continue
            s, v = p
            assert v <= self.cnt[s], f"wait on future {s}:{v}>{self.cnt[s]}"
            if v > 0:
                ws.append((s, v))
        for s, a in incs:
            self.sem(s)
            self.cnt[s] += a
        self.recs[eng].append((emit, tuple(ws), tuple(incs)))

    def val(self, s):
        self.sem(s)
        return self.cnt[s]

    def dma(self, eng, sem, dst, src, waits=()):
        def emit(e, dst=dst, src=src):
            return e.dma_start(out=dst, in_=src)
        self.add(eng, emit, waits, [(sem, 16)])
        return (sem, self.cnt[sem])

    def emit_all(self):
        nc = self.nc
        with nc.Block() as block:
            for ename in ENGINES:
                recs = self.recs[ename]
                if not recs:
                    continue

                def make_body(recs=recs):
                    def body(eng):
                        for emit, waits, incs in recs:
                            for s, v in waits:
                                eng.wait_ge(self.sems[s], v)
                            instr = emit(eng)
                            for s, a in incs:
                                instr.then_inc(self.sems[s], a)
                    return body
                getattr(block, ename)(make_body())


def build_nc(max_stage=4, sub2=None):
    if sub2 is None:
        sub2 = _SUB2
    nc = bass.Bass("TRN2", target_bir_lowering=False)

    ein = lambda n, s, d: nc.dram_tensor(n, s, d, kind="ExternalInput").ap()
    kind_scr = "ExternalOutput" if _DEBUG else "Internal"
    scr = lambda n, s, d: nc.dram_tensor(n, s, d, kind=kind_scr).ap()

    xr = ein("xr", [NKH, 128, T], F32R)
    xf = ein("xf", [NKH, 128, T], F32)
    w_in6 = ein("w_in6", [NMP, 128, NKH, 128], F32R)
    w_x6 = ein("w_x6", [NDC, 128, 288], F32R)
    w_dt6 = ein("w_dt6", [NDC, 128, 2, 128], F32R)
    w_out6 = ein("w_out6", [NMO, 128, NDC, 128], F32)
    wffn6 = ein("wffn6", [NMO, 128, NKH, 128], F32)
    lora_at = ein("lora_at", [128, 3 * NKH * NST], F32)
    lora_bt = ein("lora_bt", [16, 3 * H], F32)
    alog_t = ein("alog_t", [128, NDC * NST], F32)
    d_t = ein("d_t", [128, NDC], F32)
    convw_t = ein("convw_t", [128, NDC * KCV], F32)
    convb_t = ein("convb_t", [128, NDC], F32)
    bdt_t = ein("bdt_t", [128, NDC], F32)
    ffnb_t = ein("ffnb_t", [128, NMO], F32)
    lsi = ein("lsi", [16, 4], I32)
    cid = ein("cid", [16, 1], F32)
    ident = ein("ident", [128, 128], F32R)
    ind16 = ein("ind16", [16, NST * 128], F32R)
    onescol = ein("onescol", [128, 1], F32R)
    consts = ein("consts", [128, 4], F32)
    onesrow = ein("onesrow", [1, 128], F32R)

    xn_r = scr("xn_r", [NKH, 128, T], F32R)
    xn_b = scr("xn_b", [NKH, 128, T], BF16)
    xcv = scr("xcv", [NDC, 128, T], F32)
    sgd = scr("sgd", [NDC, 128, T], F32)
    dts = scr("dts", [NDC, 128, T], F32)
    apd = scr("apd", [NDC, 128, T], BF16)
    dbg = scr("dbg", [6, 128, T], F32)
    out = nc.dram_tensor("out", [NMO, 128, T], F32, kind="ExternalOutput").ap()

    y_ps = nc.alloc_psum_tensor("y_ps", [128, 2048], F32).ap()
    pC = nc.alloc_psum_tensor("pC", [128, 1024], F32).ap()
    pA = nc.alloc_psum_tensor("pA", [128, 512], F32).ap()
    pB = nc.alloc_psum_tensor("pB", [128, 512], F32).ap()

    cap = 206 * 1024
    SB_BASE = nc.bump_sbuf(cap)[0]
    off = [0]

    def mk(name, shape, dt, base):
        at = base[0]
        sz = int(np.prod(shape[1:])) * mybir.dt.size(dt)
        base[0] = (at + sz + 63) & ~63
        assert base[0] <= cap, f"sbuf overflow {name}: {base[0]}"
        return nc.alloc_sbuf_tensor_at(name, shape, dt,
                                       offset=SB_BASE + at).ap()

    sbP = lambda n, s, d: mk(n, s, d, off)

    s_ident = sbP("s_ident", [128, 128], F32R)
    s_ind16 = sbP("s_ind16", [16, NST * 128], F32R)
    s_onescol = sbP("s_onescol", [128, 1], F32R)
    s_const = sbP("s_const", [128, 4], F32)
    s_onesrow = sbP("s_onesrow", [1, 128], F32R)
    s_alog = sbP("s_alog", [128, NDC * NST], F32)
    s_A = sbP("s_A", [128, NDC * NST], F32)
    s_d = sbP("s_d", [128, NDC], F32)
    s_convw = sbP("s_convw", [128, NDC * KCV], F32)
    s_convb = sbP("s_convb", [128, NDC], F32)
    s_bdt = sbP("s_bdt", [128, NDC], F32)
    s_ffnb = sbP("s_ffnb", [128, NMO], F32)
    s_lsi = sbP("s_lsi", [16, 4], I32)
    s_lsif = sbP("s_lsif", [16, 4], F32)
    s_cid = sbP("s_cid", [16, 1], F32)
    s_mw = sbP("s_mw", [16, 8], F32)
    s_masks = sbP("s_masks", [16, 4], F32)
    s_Bn = sbP("s_Bn", [16, T], F32R)
    s_Cn = sbP("s_Cn", [16, T], F32R)
    P_END = off[0]
    assert P_END < 64 * 1024, P_END
    SCR = (P_END + 63) & ~63

    ctx = Ctx(nc)
    LD, GL, SS, PE, DV, AC = "LD", "GL", "SS", "PE", "DV", "AC"
    for s in (LD, GL, SS, PE, DV, AC):
        ctx.sem(s)

    def act(emit, waits=(), inc=True):
        ctx.add("scalar", emit, waits, [(AC, 1)] if inc else ())
        return (AC, ctx.val(AC))

    def dve(emit, waits=(), inc=True):
        ctx.add("vector", emit, waits, [(DV, 1)] if inc else ())
        return (DV, ctx.val(DV))

    def pe(emit, waits=(), inc=True):
        ctx.add("tensor", emit, waits, [(PE, 1)] if inc else ())
        return (PE, ctx.val(PE))

    for dst, src in [(s_ident, ident), (s_ind16, ind16), (s_onescol, onescol),
                     (s_onesrow, onesrow), (s_alog, alog_t), (s_d, d_t),
                     (s_convw, convw_t), (s_convb, convb_t), (s_bdt, bdt_t),
                     (s_ffnb, ffnb_t), (s_lsi, lsi), (s_cid, cid),
                     (s_const, consts)]:
        t_pre = ctx.dma("sync", "PRE", dst[:], src[:])

    PL = "PL"
    ctx.sem(PL)
    ctx.add("gpsimd", lambda e: e.tensor_copy(out=s_lsif[:], in_=s_lsi[:]),
            [t_pre], [(PL, 1)])
    t_pl = (PL, ctx.val(PL))
    tm = dve(lambda e: e.tensor_single_scalar(out=s_mw[:, 0:4], in_=s_lsif[:],
                                              scalar=s_cid[:], op=ALU.is_le),
             [t_pl, t_pre])
    ts_ = act(lambda e: e.activation(out=s_mw[:, 0:4], in_=s_mw[:, 0:4],
                                     func=AF.Copy,
                                     accum_out=s_mw[:, 4:5]), [tm])
    dve(lambda e: e.tensor_single_scalar(out=s_masks[:, 0:1],
                                         in_=s_mw[:, 4:5], scalar=2.0,
                                         op=ALU.is_equal), [ts_], inc=False)
    dve(lambda e: e.tensor_single_scalar(out=s_masks[:, 1:2],
                                         in_=s_mw[:, 4:5], scalar=3.0,
                                         op=ALU.is_equal), inc=False)
    dve(lambda e: e.tensor_single_scalar(out=s_masks[:, 2:3],
                                         in_=s_mw[:, 4:5], scalar=4.0,
                                         op=ALU.is_ge))
    t_masks = (DV, ctx.val(DV))
    if _DEBUG:
        ctx.dma("scalar", SS, dbg[0, 0:16, 0:4], s_masks[:], [t_masks])
        ctx.dma("scalar", SS, dbg[0, 16:32, 0:8], s_mw[:], [t_masks])

    o0 = [SCR]
    sb0 = lambda n, s, d: mk(n, s, d, o0)
    x0 = [sb0("st0_x0", [128, T], F32R), sb0("st0_x1", [128, T], F32R)]
    xf0 = [sb0("st0_xf0", [128, T], F32), sb0("st0_xf1", [128, T], F32)]
    sq0 = sb0("st0_sq", [128, T], F32R)
    sxs = sb0("st0_sxs", [1, T], F32)
    sqs = sb0("st0_sqs", [1, T], F32)
    nm0 = sb0("st0_nm", [1, T], F32)
    var0 = sb0("st0_var", [1, T], F32)
    rstd_r = sb0("st0_rstd", [1, T], F32R)
    nmr_r = sb0("st0_nmr", [1, T], F32R)
    rrep = sb0("st0_rrep", [128, T], F32)
    mrep = sb0("st0_mrep", [128, T], F32)
    xt0 = sb0("st0_xt", [128, T], F32)
    xnr0 = [sb0("st0_xnr0", [128, T], F32R), sb0("st0_xnr1", [128, T], F32R)]

    stats_banks = [pA[0:1, 0:512], pB[0:1, 0:512], pC[0:1, 0:512],
                   pC[0:1, 512:1024]]
    sq_done = {}
    mm_done = {}
    for kc in range(NKH):
        sl = kc % 2
        tld = ctx.dma("sync", f"X{sl}", x0[sl][:], xr[kc],
                      [sq_done.get(kc - 2), mm_done.get(kc - 2)])
        sq_done[kc] = act(lambda e, sl=sl: e.activation(
            out=sq0[:], in_=x0[sl][:], func=AF.Square),
            [tld, sq_done.get(kc - 1), mm_done.get(kc - 1)])
        for tt in range(4):
            pe(lambda e, sl=sl, tt=tt, kc=kc: e.matmul(
                stats_banks[tt], s_onescol[:],
                x0[sl][:, tt * 512:(tt + 1) * 512],
                start=(kc == 0), stop=(kc == NKH - 1)),
                [tld] if tt == 0 else (), inc=False)
        for tt in range(4):
            pe(lambda e, tt=tt, kc=kc: e.matmul(
                y_ps[0:1, tt * 512:(tt + 1) * 512], s_onescol[:],
                sq0[:, tt * 512:(tt + 1) * 512],
                start=(kc == 0), stop=(kc == NKH - 1)),
                [sq_done[kc]] if tt == 0 else (), inc=(tt == 3))
        mm_done[kc] = (PE, ctx.val(PE))
    t_stats_mm = mm_done[NKH - 1]
    for tt in range(4):
        act(lambda e, tt=tt: e.activation(
            out=sxs[:, tt * 512:(tt + 1) * 512], in_=stats_banks[tt],
            func=AF.Copy), [t_stats_mm] if tt == 0 else ())
    for tt in range(4):
        t_stats_ev = act(lambda e, tt=tt: e.activation(
            out=sqs[:, tt * 512:(tt + 1) * 512],
            in_=y_ps[0:1, tt * 512:(tt + 1) * 512], func=AF.Copy))
    dve(lambda e: e.tensor_scalar(out=nm0[:], in0=sxs[:], scalar1=-1.0 / H,
                                  scalar2=None, op0=ALU.mult), [t_stats_ev],
        inc=False)
    dve(lambda e: e.tensor_scalar(out=var0[:], in0=sqs[:], scalar1=1.0 / H,
                                  scalar2=None, op0=ALU.mult), inc=False)
    dve(lambda e: e.tensor_mul(out=sxs[:], in0=nm0[:], in1=nm0[:]), inc=False)
    t_var = dve(lambda e: e.tensor_sub(out=var0[:], in0=var0[:], in1=sxs[:]))
    act(lambda e: e.activation(out=var0[:], in_=var0[:], func=AF.Ln,
                               bias=s_const[0:1, 0:1]), [t_var], inc=False)
    t_rstd = act(lambda e: e.activation(out=rstd_r[:], in_=var0[:],
                                        func=AF.Exp, scale=-0.5))
    t_nmr = dve(lambda e: e.tensor_mul(out=nmr_r[:], in0=nm0[:],
                                       in1=rstd_r[:].bitcast(F32)), [t_rstd])
    last_ev = t_stats_ev
    for tt in range(4):
        tb1 = pe(lambda e, tt=tt: e.matmul(
            pA[:, 0:512], s_onesrow[:], rstd_r[:, tt * 512:(tt + 1) * 512],
            start=True, stop=True), [t_rstd, last_ev])
        e1 = act(lambda e, tt=tt: e.activation(
            out=rrep[:, tt * 512:(tt + 1) * 512], in_=pA[:, 0:512],
            func=AF.Copy), [tb1])
        tb2 = pe(lambda e, tt=tt: e.matmul(
            pA[:, 0:512], s_onesrow[:], nmr_r[:, tt * 512:(tt + 1) * 512],
            start=True, stop=True), [t_nmr, e1])
        last_ev = act(lambda e, tt=tt: e.activation(
            out=mrep[:, tt * 512:(tt + 1) * 512], in_=pA[:, 0:512],
            func=AF.Copy), [tb2])
    t_reps = last_ev
    st0_ss = {}
    st0_gl = {}
    st0_dv = {}
    for kc in range(NKH):
        sl = kc % 2
        tld = ctx.dma("sync", f"F{sl}", xf0[sl][:], xf[kc],
                      [st0_dv.get(kc - 2)])
        dve(lambda e, sl=sl: e.tensor_mul(out=xt0[:], in0=xf0[sl][:],
                                          in1=rrep[:]),
            [tld, t_reps], inc=False)
        st0_dv[kc] = dve(lambda e, sl=sl: e.tensor_add(
            out=xnr0[sl][:], in0=xt0[:], in1=mrep[:]),
            [st0_ss.get(kc - 2), st0_gl.get(kc - 2)])
        ctx.dma("scalar", f"XR{sl}", xn_r[kc], xnr0[sl][:], [st0_dv[kc]])
        st0_ss[kc] = st0_gl[kc] = ctx.dma(
            "gpsimd", f"XR{sl}", xn_b[kc],
            xnr0[sl][:].bitcast(F32), [st0_dv[kc]])

    if max_stage >= 1:
        o1 = [SCR]
        sb1 = lambda n, s, d: mk(n, s, d, o1)
        xh1 = sb1("st1_xh", [128, NKH * 1024], F32R)
        w1 = [sb1(f"st1_w{i}", [128, 2048], F32R) for i in range(2)]
        ext1 = [sb1(f"st1_ext{i}", [128, 516], F32) for i in range(2)]
        cc1 = [sb1(f"st1_cc{i}", [128, 512], F32) for i in range(2)]
        xv1 = [sb1(f"st1_xv{i}", [128, 512], F32) for i in range(2)]
        sg1 = [sb1(f"st1_sg{i}", [128, 512], F32) for i in range(2)]
        tails = sb1("st1_tails", [128, NDC * 4], F32)

        t_tail0 = dve(lambda e: e.memset(tails[:], 0.0))

        ldx1 = {}
        whalf_mm = {}
        grp_mm = {}
        ev_done = {}
        xv_ss = {}
        sg_ss = {}
        last_h_mm = None
        for h in range(2):
            st0_drain = [("XR0", ctx.val("XR0")), ("XR1", ctx.val("XR1")),
                         st0_dv[NKH - 1]]
            for kc in range(NKH):
                ldx1[(h, kc)] = ctx.dma(
                    "sync", "XH", xh1[:, kc * 1024:(kc + 1) * 1024],
                    xn_r[kc, :, h * 1024:(h + 1) * 1024].opt(),
                    (st0_drain + [last_h_mm]) if kc == 0 else ())
            for wda in range(4):
                pe(lambda e: e.matmul(pC[:, 0:128], s_ident[:], s_ident[:],
                                      start=True, stop=True),
                   [ldx1[(h, 0)]] if wda == 0 else (), inc=(wda == 3))
            for mc in range(NMP):
                g = h * NMP + mc
                ldw = {}
                for wh in range(2):
                    ldw[wh] = ctx.dma(
                        "sync", f"W{wh}", w1[wh][:],
                        w_in6[mc, :, wh * 16:(wh + 1) * 16, :].opt(),
                        [whalf_mm.get((g - 1, wh))])
                for kci in range(NKH):
                    wh = kci // 16
                    wt = w1[wh]
                    for tt in range(2):
                        ps = pA if tt == 0 else pB
                        waits = []
                        if kci == 0 and tt == 0:
                            waits = [ldw[0], ldx1[(h, NKH - 1)],
                                     ev_done.get((g - 1, 0)),
                                     ev_done.get((g - 1, 1))]
                        elif kci == 16 and tt == 0:
                            waits = [ldw[1]]
                        pe(lambda e, ps=ps, wt=wt, kci=kci, tt=tt: e.matmul(
                            ps[:, 0:512],
                            wt[:, (kci % 16) * 128:((kci % 16) + 1) * 128],
                            xh1[:, kci * 1024 + tt * 512:kci * 1024 + tt * 512 + 512],
                            start=(kci == 0), stop=(kci == NKH - 1)),
                            waits, inc=(kci == NKH - 1 or (kci == 15 and tt == 1)))
                    if kci == 15:
                        whalf_mm[(g, 0)] = (PE, ctx.val(PE))
                pe(lambda e: e.drain(), (), inc=True)
                grp_mm[(g, 0)] = grp_mm[(g, 1)] = whalf_mm[(g, 1)] = \
                    (PE, ctx.val(PE))
                for tt in range(2):
                    ps = pA if tt == 0 else pB
                    tmm = grp_mm[(g, tt)]
                    gt = 2 * h + tt
                    es = gt % 2
                    if mc < NDC:
                        dc = mc
                        tac1 = dve(lambda e, ps=ps, es=es: e.tensor_copy(
                            out=ext1[es][:, 4:516], in_=ps[:, 0:512]), [tmm])
                        ev_done[(g, tt)] = tac1
                        dve(lambda e, es=es, dc=dc: e.tensor_copy(
                            out=ext1[es][:, 1:4],
                            in_=tails[:, dc * 4 + 1:dc * 4 + 4]),
                            [t_tail0], inc=False)
                        dve(lambda e, es=es, dc=dc: e.tensor_copy(
                            out=tails[:, dc * 4 + 1:dc * 4 + 4],
                            in_=ext1[es][:, 513:516]), inc=False)
                        dve(lambda e, es=es, dc=dc: e.tensor_scalar(
                            out=cc1[es][:], in0=ext1[es][:, 1:513],
                            scalar1=s_convw[:, dc * 4:dc * 4 + 1],
                            scalar2=s_convb[:, dc:dc + 1],
                            op0=ALU.mult, op1=ALU.add), inc=False)
                        for j in range(1, 4):
                            dve(lambda e, es=es, dc=dc, j=j:
                                e.scalar_tensor_tensor(
                                    out=cc1[es][:], in0=ext1[es][:, 1 + j:513 + j],
                                    scalar=s_convw[:, dc * 4 + j:dc * 4 + j + 1],
                                    in1=cc1[es][:], op0=ALU.mult, op1=ALU.add),
                                inc=(j == 3))
                        tcv = (DV, ctx.val(DV))
                        tsil = act(lambda e, es=es: e.activation(
                            out=xv1[es][:], in_=cc1[es][:], func=AF.Silu),
                            [tcv, xv_ss.get(es)])
                        xv_ss[es] = ctx.dma(
                            "scalar", f"XV{es}",
                            xcv[dc, :, gt * 512:(gt + 1) * 512].opt(), xv1[es][:],
                            [tsil])
                    else:
                        gc = mc - NDC
                        tev = act(lambda e, ps=ps, es=es: e.activation(
                            out=sg1[es][:], in_=ps[:, 0:512], func=AF.Silu),
                            [tmm, sg_ss.get(es)])
                        ev_done[(g, tt)] = tev
                        sg_ss[es] = ctx.dma(
                            "scalar", f"SG{es}",
                            sgd[gc, :, gt * 512:(gt + 1) * 512].opt(), sg1[es][:],
                            [tev])
            last_h_mm = grp_mm[(h * NMP + NMP - 1, 1)]
        t_st1_ss = (SS, ctx.val(SS))

    if max_stage >= 2:
        o2 = [SCR]
        sb2 = lambda n, s, d: mk(n, s, d, o2)
        xcr2 = [sb2(f"st2_xcr{i}", [128, 512], F32R) for i in range(2)]
        wx2 = [sb2(f"st2_wx{i}", [128, 288], F32R) for i in range(2)]
        ssm0 = sb2("st2_ssm0", [128, T], F32)
        ssm1 = sb2("st2_ssm1", [128, T], F32)
        ssm2 = sb2("st2_ssm2", [16, 2 * T], F32)
        sqr2 = sb2("st2_sqr", [128, T], F32R)
        tsrep = sb2("st2_tsrep", [128, T], F32)
        tsn = [sb2(f"st2_tsn{i}", [128, T], F32R) for i in range(2)]
        strow = sb2("st2_strow", [1, T], F32)
        rstd2 = sb2("st2_rstd2", [1, T], F32R)
        srowB = sb2("st2_srowB", [1, T], F32)
        srowC = sb2("st2_srowC", [1, T], F32)
        rstdB = sb2("st2_rstdB", [1, T], F32R)
        rstdC = sb2("st2_rstdC", [1, T], F32R)
        repB = sb2("st2_repB", [16, T], F32)
        repC = sb2("st2_repC", [16, T], F32)
        wdt2 = [sb2(f"st2_wdt{i}", [128, 256], F32R) for i in range(2)]
        sp_t = [sb2(f"st2_spt{i}", [128, 512], F32) for i in range(2)]
        sp_m = [sb2(f"st2_spm{i}", [128, 512], F32) for i in range(2)]
        dt_t = [sb2(f"st2_dt{i}", [128, 512], F32) for i in range(2)]

        mm_1b = {}
        ev_1b = None
        for tt in range(4):
            for dc in range(NDC):
                sl = dc % 2
                g = tt * NDC + dc
                lw = [mm_1b.get(g - 2)]
                if g == 0:
                    lw += [("XV0", ctx.val("XV0")), ("XV1", ctx.val("XV1")),
                           ("SG0", ctx.val("SG0")), ("SG1", ctx.val("SG1"))]
                tx = ctx.dma("sync", f"XC{sl}", xcr2[sl][:],
                             xcv[dc, :, tt * 512:(tt + 1) * 512]
                             .bitcast(F32R).opt(), lw)
                tw = ctx.dma("sync", f"WX{sl}", wx2[sl][:], w_x6[dc], ())
                pewaits = [tx, tw]
                if dc == 0 and tt > 0:
                    pewaits.append(ev_1b)
                pe(lambda e, sl=sl, dc=dc: e.matmul(
                    pA[:, 0:512], wx2[sl][:, 0:128], xcr2[sl][:],
                    start=(dc == 0), stop=(dc == NDC - 1)), pewaits, inc=False)
                pe(lambda e, sl=sl, dc=dc: e.matmul(
                    pB[:, 0:512], wx2[sl][:, 128:256], xcr2[sl][:],
                    start=(dc == 0), stop=(dc == NDC - 1)), inc=False)
                pe(lambda e, sl=sl, dc=dc: e.matmul(
                    pC[0:16, 0:512], wx2[sl][:, 256:272], xcr2[sl][:],
                    start=(dc == 0), stop=(dc == NDC - 1)), inc=False)
                pe(lambda e, sl=sl, dc=dc: e.matmul(
                    pC[0:16, 512:1024], wx2[sl][:, 272:288], xcr2[sl][:],
                    start=(dc == 0), stop=(dc == NDC - 1)), inc=True)
                mm_1b[g] = (PE, ctx.val(PE))
            tmm = mm_1b[tt * NDC + NDC - 1]
            act(lambda e, tt=tt: e.activation(
                out=ssm0[:, tt * 512:(tt + 1) * 512], in_=pA[:, 0:512],
                func=AF.Copy), [tmm], inc=False)
            act(lambda e, tt=tt: e.activation(
                out=ssm1[:, tt * 512:(tt + 1) * 512], in_=pB[:, 0:512],
                func=AF.Copy), inc=False)
            act(lambda e, tt=tt: e.activation(
                out=ssm2[:, tt * 512:(tt + 1) * 512], in_=pC[0:16, 0:512],
                func=AF.Copy), inc=False)
            ev_1b = act(lambda e, tt=tt: e.activation(
                out=ssm2[:, T + tt * 512:T + (tt + 1) * 512],
                in_=pC[0:16, 512:1024], func=AF.Copy))
        t_ssm = ev_1b

        SUB2 = sub2
        if SUB2 >= 1:
            tsq = None
            t_mm_prev = None
            for kci, smt in enumerate((ssm0, ssm1)):
                tsq = act(lambda e, smt=smt: e.activation(
                    out=sqr2[:], in_=smt[:], func=AF.Square),
                    [t_ssm] if kci == 0 else [t_mm_prev])
                for tt in range(4):
                    pe(lambda e, tt=tt, kci=kci: e.matmul(
                        stats_banks[tt], s_onescol[:],
                        sqr2[:, tt * 512:(tt + 1) * 512],
                        start=(kci == 0), stop=(kci == 1)),
                        [tsq] if tt == 0 else (), inc=(tt == 3))
                t_mm_prev = (PE, ctx.val(PE))
            t_tsmm = (PE, ctx.val(PE))
            for tt in range(4):
                t_tsev = act(lambda e, tt=tt: e.activation(
                    out=strow[:, tt * 512:(tt + 1) * 512], in_=stats_banks[tt],
                    func=AF.Copy), [t_tsmm] if tt == 0 else ())
            act(lambda e: e.activation(out=strow[:], in_=strow[:], func=AF.Ln,
                                       bias=s_const[0:1, 1:2], scale=1.0 / R),
                [t_tsev], inc=False)
            t_rstd2 = act(lambda e: e.activation(out=rstd2[:], in_=strow[:],
                                                 func=AF.Exp, scale=-0.5))
            last = t_rstd2
            for tt in range(4):
                tb1 = pe(lambda e, tt=tt: e.matmul(
                    pA[:, 0:512], s_onesrow[:], rstd2[:, tt * 512:(tt + 1) * 512],
                    start=True, stop=True), [t_rstd2, last])
                last = dve(lambda e, tt=tt: e.tensor_copy(
                    out=tsrep[:, tt * 512:(tt + 1) * 512], in_=pA[:, 0:512]),
                    [tb1])
            t_tsrep = last
            dve(lambda e: e.tensor_mul(out=tsn[0][:], in0=ssm0[:], in1=tsrep[:]),
                [t_tsrep], inc=False)
            t_tsn = dve(lambda e: e.tensor_mul(out=tsn[1][:], in0=ssm1[:],
                                               in1=tsrep[:]))

        if SUB2 >= 2:
            tBsq = act(lambda e: e.activation(out=sqr2[0:16, :], in_=ssm2[:, 0:T],
                                              func=AF.Square), [t_tsrep])
            for tt in range(4):
                pe(lambda e, tt=tt: e.matmul(
                    stats_banks[tt], s_onescol[0:16, :],
                    sqr2[0:16, tt * 512:(tt + 1) * 512], start=True, stop=True),
                    [tBsq] if tt == 0 else (), inc=(tt == 3))
            tBmm = (PE, ctx.val(PE))
            for tt in range(4):
                tBev = act(lambda e, tt=tt: e.activation(
                    out=srowB[:, tt * 512:(tt + 1) * 512], in_=stats_banks[tt],
                    func=AF.Copy), [tBmm] if tt == 0 else ())
            tCsq = act(lambda e: e.activation(out=sqr2[0:16, :], in_=ssm2[:, T:2 * T],
                                              func=AF.Square), [tBev])
            for tt in range(4):
                pe(lambda e, tt=tt: e.matmul(
                    stats_banks[tt], s_onescol[0:16, :],
                    sqr2[0:16, tt * 512:(tt + 1) * 512], start=True, stop=True),
                    [tCsq] if tt == 0 else (), inc=(tt == 3))
            tCmm = (PE, ctx.val(PE))
            for tt in range(4):
                tCev = act(lambda e, tt=tt: e.activation(
                    out=srowC[:, tt * 512:(tt + 1) * 512], in_=stats_banks[tt],
                    func=AF.Copy), [tCmm] if tt == 0 else ())
            act(lambda e: e.activation(out=srowB[:], in_=srowB[:], func=AF.Ln,
                                       bias=s_const[0:1, 1:2], scale=1.0 / NST),
                [tCev], inc=False)
            act(lambda e: e.activation(out=rstdB[:], in_=srowB[:], func=AF.Exp,
                                       scale=-0.5), inc=False)
            act(lambda e: e.activation(out=srowC[:], in_=srowC[:], func=AF.Ln,
                                       bias=s_const[0:1, 1:2], scale=1.0 / NST),
                inc=False)
            t_rstdBC = act(lambda e: e.activation(out=rstdC[:], in_=srowC[:],
                                                  func=AF.Exp, scale=-0.5))
            last = t_rstdBC
            for tt in range(4):
                tb1 = pe(lambda e, tt=tt: e.matmul(
                    pA[0:16, 0:512], s_onesrow[:, 0:16],
                    rstdB[:, tt * 512:(tt + 1) * 512], start=True, stop=True),
                    [t_rstdBC, last])
                eB = dve(lambda e, tt=tt: e.tensor_copy(
                    out=repB[:, tt * 512:(tt + 1) * 512], in_=pA[0:16, 0:512]),
                    [tb1])
                tb2 = pe(lambda e, tt=tt: e.matmul(
                    pB[0:16, 0:512], s_onesrow[:, 0:16],
                    rstdC[:, tt * 512:(tt + 1) * 512], start=True, stop=True), [eB])
                last = dve(lambda e, tt=tt: e.tensor_copy(
                    out=repC[:, tt * 512:(tt + 1) * 512], in_=pB[0:16, 0:512]),
                    [tb2])
            t_repBC = last
            dve(lambda e: e.tensor_mul(out=s_Bn[:], in0=ssm2[:, 0:T], in1=repB[:]),
                [t_repBC], inc=False)
            t_BCn = dve(lambda e: e.tensor_mul(out=s_Cn[:], in0=ssm2[:, T:2 * T],
                                               in1=repC[:]))
            if _DEBUG and _TAPS:
                ctx.dma("scalar", SS, dbg[0], ssm0[:], [t_BCn])
                ctx.dma("scalar", SS, dbg[1], ssm1[:], [t_BCn])
                ctx.dma("scalar", SS, dbg[2], tsrep[:], [t_BCn])
                ctx.dma("scalar", SS, dbg[3], tsn[0][:].bitcast(F32), [t_BCn])
                ctx.dma("scalar", SS, dbg[4], tsn[1][:].bitcast(F32), [t_BCn])
                ctx.dma("scalar", SS, dbg[5, 0:1, :], strow[:], [t_BCn])
                ctx.dma("scalar", SS, dbg[5, 1:2, :], rstd2[:].bitcast(F32),
                        [t_BCn])
                ctx.dma("scalar", SS, dbg[5, 2:3, :], s_Bn[0:1, :].bitcast(F32),
                        [t_BCn])

        if SUB2 >= 3:
            wdt_mm = {}
            dt_ps = {}
            dt_ss = {}
            for mc in range(NDC):
                sl = mc % 2
                twd = ctx.dma("sync", f"WD{sl}", wdt2[sl][:],
                              w_dt6[mc].opt(), [wdt_mm.get(mc - 2)])
                for tt in range(4):
                    gseq = mc * 4 + tt
                    ps = pA if tt % 2 == 0 else pB
                    prev = dt_ps.get(gseq - 2)
                    waits = [twd, t_tsn, t_repBC]
                    if prev is not None:
                        waits += [prev[0], prev[1]]
                    for kci in range(2):
                        pe(lambda e, sl=sl, kci=kci, tt=tt, ps=ps: e.matmul(
                            ps[:, 0:512], wdt2[sl][:, kci * 128:(kci + 1) * 128],
                            tsn[kci][:, tt * 512:(tt + 1) * 512],
                            start=(kci == 0), stop=(kci == 1)),
                            waits if kci == 0 else (), inc=(kci == 1))
                    tmm = (PE, ctx.val(PE))
                    wdt_mm[mc] = tmm
                    es = tt % 2
                    wprev = [tmm, dt_ps.get(gseq - 2, ((DV, 0),))[-1]]
                    if sub2 >= 5:
                        a2 = act(lambda e, ps=ps, es=es: e.activation(
                            out=dt_t[es][:], in_=ps[:, 0:512], func=AF.Copy),
                            [tmm, dt_ss.get(gseq - 2)])
                        dt_ps[gseq] = (a2, a2)
                        dt_ss[gseq] = ctx.dma(
                            "scalar", f"DT{es}",
                            dts[mc, :, tt * 512:(tt + 1) * 512].opt(),
                            dt_t[es][:], [a2])
                        continue
                    if sub2 >= 4:
                        a1 = act(lambda e, ps=ps, es=es: e.activation(
                            out=sp_t[es][:], in_=ps[:, 0:512], func=AF.Square),
                            wprev)
                        a2 = a1
                    else:
                        a1 = act(lambda e, ps=ps, es=es, mc=mc: e.activation(
                            out=sp_t[es][:], in_=ps[:, 0:512], func=AF.Abs,
                            bias=s_bdt[:, mc:mc + 1]), wprev)
                        act(lambda e, es=es: e.activation(
                            out=sp_t[es][:], in_=sp_t[es][:], func=AF.Exp,
                            scale=-1.0), inc=False)
                        a2 = act(lambda e, es=es: e.activation(
                            out=sp_t[es][:], in_=sp_t[es][:], func=AF.Ln,
                            bias=s_const[:, 2:3]))
                    dve(lambda e, ps=ps, es=es, mc=mc: e.tensor_scalar(
                        out=sp_m[es][:], in0=ps[:, 0:512],
                        scalar1=s_bdt[:, mc:mc + 1], scalar2=0.0,
                        op0=ALU.add, op1=ALU.max), [a1], inc=False)
                    d1 = dve(lambda e, es=es: e.tensor_add(
                        out=dt_t[es][:], in0=sp_m[es][:], in1=sp_t[es][:]),
                        [a2, dt_ss.get(gseq - 2)])
                    dt_ps[gseq] = (a2, d1)
                    dt_ss[gseq] = ctx.dma(
                        "scalar", f"DT{es}",
                        dts[mc, :, tt * 512:(tt + 1) * 512].opt(),
                        dt_t[es][:], [d1])
        t_st2_done = [("DT0", ctx.val("DT0")), ("DT1", ctx.val("DT1")),
                      ("XV0", ctx.val("XV0")), ("XV1", ctx.val("XV1")),
                      ("SG0", ctx.val("SG0")), ("SG1", ctx.val("SG1"))]

    if max_stage >= 3:
        o3 = [SCR]
        sb3 = lambda n, s, d: mk(n, s, d, o3)
        dt3 = [sb3(f"st3_dt{i}", [128, T], F32) for i in range(2)]
        u3 = [sb3(f"st3_u{i}", [128, T], F32) for i in range(2)]
        sg3 = [sb3(f"st3_sg{i}", [128, T], F32) for i in range(2)]
        db3 = [sb3(f"st3_db{i}", [128, T], F32) for i in range(2)]
        dA3 = [sb3(f"st3_dA{i}", [128, T], F32) for i in range(2)]
        bb3 = sb3("st3_bb", [128, T], F32)
        stt3 = sb3("st3_stt", [128, T], F32)
        pr3 = [sb3(f"st3_pr{i}", [128, 1024], F32R) for i in range(2)]
        e13 = sb3("st3_e1", [128, T], F32)
        ap3 = [sb3(f"st3_ap{i}", [128, T], BF16) for i in range(2)]

        tA1 = act(lambda e: e.activation(out=s_A[:], in_=s_alog[:], func=AF.Exp),
                  [t_pre])
        tA2 = dve(lambda e: e.tensor_scalar(out=s_A[:], in0=s_A[:], scalar1=-1.0,
                                            scalar2=None, op0=ALU.mult), [tA1])

        scn = {}
        dcdone = {}
        ap_ss = {}
        t_pr_free = [t_BCn, t_BCn]
        t_yps_free = t_BCn
        pr_pe = [(PE, 0), (PE, 0)]
        t_pC_free = t_BCn
        for dc in range(NDC):
            sl = dc % 2
            lw = [dcdone.get(dc - 2)]
            if dc == 0:
                lw += t_st2_done
            ctx.dma("sync", f"SC{sl}", dt3[sl][:], dts[dc], lw)
            ctx.dma("sync", f"SC{sl}", u3[sl][:], xcv[dc], ())
            l3 = l1 = ctx.dma("sync", f"SC{sl}", sg3[sl][:], sgd[dc], ())
            tdb = dve(lambda e, sl=sl: e.tensor_mul(
                out=db3[sl][:], in0=dt3[sl][:], in1=u3[sl][:]),
                [l1, tA2, t_BCn])
            for n in range(NST):
                asl = n % 2
                wda = [l1, tA2]
                if n >= 2:
                    wda.append(scn[(dc, n - 2)])
                elif dc >= 1:
                    wda.append(dcdone.get(dc - 1))
                tda = act(lambda e, sl=sl, asl=asl, dc=dc, n=n: e.activation(
                    out=dA3[asl][:], in_=dt3[sl][:], func=AF.Exp,
                    scale=s_A[:, dc * NST + n:dc * NST + n + 1]), wda)
                pe(lambda e, n=n: e.matmul(
                    pC[:, 0:512], s_ind16[:, n * 128:(n + 1) * 128],
                    s_Bn[:, 0:512], start=True, stop=True),
                    [t_pC_free], inc=False)
                tbc = pe(lambda e, n=n: e.matmul(
                    pC[:, 512:1024], s_ind16[:, n * 128:(n + 1) * 128],
                    s_Bn[:, 512:1024], start=True, stop=True))
                tbb0 = dve(lambda e, sl=sl: e.tensor_mul(
                    out=bb3[:, 0:1024], in0=db3[sl][:, 0:1024], in1=pC[:, 0:1024]),
                    [tbc, tdb])
                pe(lambda e, n=n: e.matmul(
                    pC[:, 0:512], s_ind16[:, n * 128:(n + 1) * 128],
                    s_Bn[:, 1024:1536], start=True, stop=True), [tbb0],
                    inc=False)
                tbc2 = pe(lambda e, n=n: e.matmul(
                    pC[:, 512:1024], s_ind16[:, n * 128:(n + 1) * 128],
                    s_Bn[:, 1536:2048], start=True, stop=True))
                tbb1 = dve(lambda e, sl=sl: e.tensor_mul(
                    out=bb3[:, 1024:2048], in0=db3[sl][:, 1024:2048],
                    in1=pC[:, 0:1024]), [tbc2])
                tscan = dve(lambda e, asl=asl: e.tensor_tensor_scan(
                    out=stt3[:], data0=dA3[asl][:], data1=bb3[:], initial=0.0,
                    op0=ALU.mult, op1=ALU.add), [tda])
                scn[(dc, n)] = tscan
                tprv = tbb1
                for hh in range(2):
                    pe(lambda e, hh=hh, n=n: e.matmul(
                        pC[:, 0:512], s_ind16[:, n * 128:(n + 1) * 128],
                        s_Cn[:, hh * 1024:hh * 1024 + 512], start=True,
                        stop=True), [tprv], inc=False)
                    tbcC = pe(lambda e, hh=hh, n=n: e.matmul(
                        pC[:, 512:1024], s_ind16[:, n * 128:(n + 1) * 128],
                        s_Cn[:, hh * 1024 + 512:hh * 1024 + 1024], start=True,
                        stop=True))
                    wpr = [tbcC, tscan, pr_pe[hh]]
                    if n == 0 and hh == 0:
                        wpr.append(t_yps_free)
                    tpr = dve(lambda e, hh=hh: e.tensor_mul(
                        out=pr3[hh][:], in0=stt3[:, hh * 1024:(hh + 1) * 1024],
                        in1=pC[:, 0:1024]), wpr)
                    tprv = tpr
                    for q in range(2):
                        pe(lambda e, hh=hh, q=q, n=n: e.matmul(
                            y_ps[:, hh * 1024 + q * 512:hh * 1024 + q * 512 + 512],
                            s_ident[:], pr3[hh][:, q * 512:(q + 1) * 512],
                            start=(n == 0), stop=(n == NST - 1)),
                            [tpr] if q == 0 else (), inc=(q == 1))
                    pr_pe[hh] = (PE, ctx.val(PE))
                t_pC_free = tprv
            t_ylast = pr_pe[1]
            dve(lambda e, sl=sl, dc=dc: e.scalar_tensor_tensor(
                out=e13[:], in0=u3[sl][:], scalar=s_d[:, dc:dc + 1],
                in1=y_ps[:], op0=ALU.mult, op1=ALU.add),
                [t_ylast, l3], inc=False)
            t_yps_free = dve(lambda e, dc=dc, sl=sl: e.tensor_mul(
                out=ap3[dc % 2][:], in0=e13[:], in1=sg3[sl][:]),
                [ap_ss.get(dc - 2)])
            dcdone[dc] = t_yps_free
            ap_ss[dc] = ctx.dma("scalar", f"AP{dc % 2}", apd[dc],
                                ap3[dc % 2][:], [dcdone[dc]])
        t_st3_done = [("AP0", ctx.val("AP0")), ("AP1", ctx.val("AP1"))]

    if max_stage >= 4:
        o4 = [SCR]
        sb4 = lambda n, s, d: mk(n, s, d, o4)
        ap4 = sb4("st4_ap", [128, NDC * 512], BF16)
        xnb4 = sb4("st4_xnb", [128, NKH * 512], BF16)
        wout4 = sb4("st4_wout", [128, NDC * 128], BF16)
        wffn4 = sb4("st4_wffn", [128, NKH * 128], BF16)
        lA4 = sb4("st4_lA", [128, 3 * NKH * NST], BF16)
        lB4 = sb4("st4_lB", [16, 3 * H], BF16)
        skm4 = sb4("st4_skm", [16, 3 * 512], BF16)
        xf4 = [sb4(f"st4_xf{i}", [128, 512], F32) for i in range(2)]
        o14 = [sb4(f"st4_o1{i}", [128, 512], F32) for i in range(2)]
        ot4 = [sb4(f"st4_ot{i}", [128, 512], F32) for i in range(2)]

        tlA = ctx.dma("gpsimd", GL, lA4[:], lora_at[:], [dcdone.get(1)])
        tlB = ctx.dma("gpsimd", GL, lB4[:], lora_bt[:], ())

        last_wmm = (PE, 0)
        sk_dv = None
        ev4 = {}
        out_ss = {}
        for tb in range(4):
            lap = None
            for dc in range(NDC):
                ws = ()
                if dc == 0:
                    ws = t_st3_done + [last_wmm]
                    if tb == 0:
                        ws += [("XR0", ctx.val("XR0")),
                               ("XR1", ctx.val("XR1"))]
                lap = ctx.dma("sync", "L4", ap4[:, dc * 512:(dc + 1) * 512],
                              apd[dc, :, tb * 512:(tb + 1) * 512].opt(), ws)
            for kc in range(NKH):
                lap = ctx.dma("sync", "L4", xnb4[:, kc * 512:(kc + 1) * 512],
                              xn_b[kc, :, tb * 512:(tb + 1) * 512].opt(), ())
            t_ld4 = lap
            sk_ps = [pC[0:16, 0:512], pC[0:16, 512:1024], y_ps[0:16, 0:512]]
            if tb == 0:
                wlora = [tlB, t_ld4, t_masks, t_pC_free, t_yps_free]
            else:
                wlora = [t_ld4, sk_dv]
            for k in range(3):
                for kc in range(NKH):
                    pe(lambda e, k=k, kc=kc: e.matmul(
                        sk_ps[k],
                        lA4[:, (k * NKH + kc) * NST:(k * NKH + kc + 1) * NST],
                        xnb4[:, kc * 512:(kc + 1) * 512],
                        start=(kc == 0), stop=(kc == NKH - 1)),
                        wlora if (k == 0 and kc == 0) else (),
                        inc=(kc == NKH - 1))
            t_skmm = (PE, ctx.val(PE))
            for k in range(3):
                sk_dv = dve(lambda e, k=k: e.tensor_scalar(
                    out=skm4[:, k * 512:(k + 1) * 512], in0=sk_ps[k],
                    scalar1=s_masks[:, k:k + 1], scalar2=None, op0=ALU.mult),
                    [t_skmm, tlB] if k == 0 else ())
            for mc in range(NMO):
                sl = mc % 2
                g4 = tb * NMO + mc
                two = ctx.dma("gpsimd", GL, wout4[:], w_out6[mc].opt(),
                              [sk_dv, last_wmm])
                twf = ctx.dma("gpsimd", GL, wffn4[:], wffn6[mc].opt(), ())
                txr = ctx.dma("sync", f"XF{sl}", xf4[sl][:],
                              xf[mc, :, tb * 512:(tb + 1) * 512].opt(),
                              [ev4.get(g4 - 2)])
                ps = pA if sl == 0 else pB
                waits = [twf, sk_dv, ev4.get(g4 - 2)]
                for kci in range(NDC):
                    pe(lambda e, kci=kci, ps=ps: e.matmul(
                        ps[:, 0:512], wout4[:, kci * 128:(kci + 1) * 128],
                        ap4[:, kci * 512:(kci + 1) * 512],
                        start=(kci == 0), stop=False),
                        waits if kci == 0 else (), inc=False)
                for kci in range(NKH):
                    pe(lambda e, kci=kci, ps=ps: e.matmul(
                        ps[:, 0:512], wffn4[:, kci * 128:(kci + 1) * 128],
                        xnb4[:, kci * 512:(kci + 1) * 512],
                        start=False, stop=False), inc=False)
                for k in range(3):
                    pe(lambda e, k=k, mc=mc, ps=ps: e.matmul(
                        ps[:, 0:512],
                        lB4[:, k * H + mc * 128:k * H + (mc + 1) * 128],
                        skm4[:, k * 512:(k + 1) * 512],
                        start=False, stop=(k == 2)), inc=(k == 2))
                last_wmm = (PE, ctx.val(PE))
                dve(lambda e, ps=ps, mc=mc, sl=sl: e.tensor_scalar(
                    out=o14[sl][:], in0=ps[:, 0:512],
                    scalar1=s_ffnb[:, mc:mc + 1], scalar2=None, op0=ALU.add),
                    [last_wmm, txr, out_ss.get(g4 - 2)], inc=False)
                ev4[g4] = dve(lambda e, sl=sl: e.tensor_add(
                    out=ot4[sl][:], in0=o14[sl][:], in1=xf4[sl][:]))
                out_ss[g4] = ctx.dma(
                    "scalar", f"OT{sl}",
                    out[mc, :, tb * 512:(tb + 1) * 512].opt(),
                    ot4[sl][:], [ev4[g4]])
    t_fin0 = ctx.val("OT0")
    t_fin1 = ctx.val("OT1")

    def fin(e, v0=t_fin0, v1=t_fin1):
        e.wait_ge(ctx.sems["OT0"], v0)
        e.wait_ge(ctx.sems["OT1"], v1)
        return None
    ctx.add("sync", fin)
    ctx.add("gpsimd", fin)

    ctx.emit_all()
    return nc


def _prep_inputs(inputs):
    x = np.asarray(inputs["L0_attn_hidden_states"], np.float32)
    W_in = np.asarray(inputs["L0_attn_W_in"], np.float32)
    conv_w = np.asarray(inputs["L0_attn_conv_w"], np.float32)
    conv_b = np.asarray(inputs["L0_attn_conv_b"], np.float32)
    W_x = np.asarray(inputs["L0_attn_W_x"], np.float32)
    W_dt = np.asarray(inputs["L0_attn_W_dt"], np.float32)
    b_dt = np.asarray(inputs["L0_attn_b_dt"], np.float32)
    A_log = np.asarray(inputs["L0_attn_A_log"], np.float32)
    D = np.asarray(inputs["L0_attn_D"], np.float32)
    W_out = np.asarray(inputs["L0_attn_W_out"], np.float32)
    Wf = np.asarray(inputs["L0_ffn_weight"], np.float32)
    fb = np.asarray(inputs["L0_ffn_bias"], np.float32)
    lA = np.asarray(inputs["L0_ffn_lora_A"], np.float32)
    lB = np.asarray(inputs["L0_ffn_lora_B"], np.float32)
    lsi_v = np.asarray(inputs["L0_ffn_lora_start_indices"]).astype(np.int32)

    C = np.ascontiguousarray
    ind = np.zeros((NST, NST * 128), np.float32)
    for n in range(NST):
        ind[n, n * 128:(n + 1) * 128] = 1.0
    shared = {
        "w_in6": C(W_in.T.reshape(NKH, 128, NMP, 128).transpose(2, 1, 0, 3)),
        "w_x6": C(W_x.T.reshape(NDC, 128, 288)),
        "w_dt6": C(W_dt.T.reshape(2, 128, NDC, 128).transpose(2, 1, 0, 3)),
        "w_out6": C(W_out.T.reshape(NDC, 128, NMO, 128).transpose(2, 1, 0, 3)),
        "wffn6": C(Wf.T.reshape(NKH, 128, NMO, 128).transpose(2, 1, 0, 3)),
        "lora_at": C(lA.transpose(0, 2, 1).reshape(3, NKH, 128, NST)
                     .transpose(2, 0, 1, 3).reshape(128, 3 * NKH * NST)),
        "lora_bt": C(lB.transpose(2, 0, 1).reshape(16, 3 * H)),
        "alog_t": C(A_log.reshape(NDC, 128, NST).transpose(1, 0, 2)
                    .reshape(128, NDC * NST)),
        "d_t": C(D.reshape(NDC, 128).T),
        "convw_t": C(conv_w.reshape(NDC, 128, KCV).transpose(1, 0, 2)
                     .reshape(128, NDC * KCV)),
        "convb_t": C(conv_b.reshape(NDC, 128).T),
        "bdt_t": C(b_dt.reshape(NDC, 128).T),
        "ffnb_t": C(fb.reshape(NMO, 128).T),
        "lsi": C(np.tile(lsi_v.reshape(1, 4), (16, 1))),
        "ident": np.eye(128, dtype=np.float32),
        "ind16": ind,
        "onescol": np.ones((128, 1), np.float32),
        "consts": np.tile(np.array([[1e-5, 1e-6, 1.0, 0.0]], np.float32),
                          (128, 1)),
        "onesrow": np.ones((1, 128), np.float32),
    }
    in_maps = []
    for i in range(NB):
        xi = C(x[i].T.reshape(NKH, 128, T))
        m = dict(shared)
        m["xr"] = xi
        m["xf"] = xi
        m["cid"] = np.full((16, 1), float(i), np.float32)
        in_maps.append(m)
    return in_maps


def kernel(**inputs):
    in_maps = _prep_inputs(inputs)
    nc = build_nc()
    res = run_bass_kernel_spmd(nc, in_maps, core_ids=list(range(NB)))
    outs = []
    for r in res.results:
        outs.append(r["out"].reshape(H, T).T)
    return np.stack(outs, axis=0).astype(np.float32)


# version 06
# speedup vs baseline: 1.2484x; 1.2484x over previous
import numpy as np
import concourse.bass as bass
import concourse.mybir as mybir
from concourse.bass_utils import run_bass_kernel_spmd

F32 = mybir.dt.float32
F32R = mybir.dt.float32r
BF16 = mybir.dt.bfloat16
I32 = mybir.dt.int32
ALU = mybir.AluOpType
AF = mybir.ActivationFunctionType

H = 4096
DIN = 8192
NST = 16
KCV = 4
R = 256
T = 2048
NB = 8
NKH = H // 128
NDC = DIN // 128
NMP = 2 * DIN // 128
NMO = H // 128

_DEBUG = False
_TAPS = False
_SUB2 = 3
ENGINES = ("sync", "vector", "scalar", "tensor", "gpsimd")


class Ctx:

    def __init__(self, nc):
        self.nc = nc
        self.recs = {e: [] for e in ENGINES}
        self.sems = {}
        self.cnt = {}

    def sem(self, name):
        if name not in self.sems:
            self.sems[name] = self.nc.semaphore(name).__enter__()
            self.cnt[name] = 0
        return self.sems[name]

    def add(self, eng, emit, waits=(), incs=()):
        ws = []
        for p in waits:
            if p is None:
                continue
            s, v = p
            assert v <= self.cnt[s], f"wait on future {s}:{v}>{self.cnt[s]}"
            if v > 0:
                ws.append((s, v))
        for s, a in incs:
            self.sem(s)
            self.cnt[s] += a
        self.recs[eng].append((emit, tuple(ws), tuple(incs)))

    def val(self, s):
        self.sem(s)
        return self.cnt[s]

    def dma(self, eng, sem, dst, src, waits=()):
        def emit(e, dst=dst, src=src):
            return e.dma_start(out=dst, in_=src)
        self.add(eng, emit, waits, [(sem, 16)])
        return (sem, self.cnt[sem])

    def emit_all(self):
        nc = self.nc
        with nc.Block() as block:
            for ename in ENGINES:
                recs = self.recs[ename]
                if not recs:
                    continue

                def make_body(recs=recs):
                    def body(eng):
                        for emit, waits, incs in recs:
                            for s, v in waits:
                                eng.wait_ge(self.sems[s], v)
                            instr = emit(eng)
                            for s, a in incs:
                                instr.then_inc(self.sems[s], a)
                    return body
                getattr(block, ename)(make_body())


def build_nc(max_stage=4, sub2=None):
    if sub2 is None:
        sub2 = _SUB2
    nc = bass.Bass("TRN2", target_bir_lowering=False)

    ein = lambda n, s, d: nc.dram_tensor(n, s, d, kind="ExternalInput").ap()
    kind_scr = "ExternalOutput" if _DEBUG else "Internal"
    scr = lambda n, s, d: nc.dram_tensor(n, s, d, kind=kind_scr).ap()

    xr = ein("xr", [NKH, 128, T], F32R)
    xf = ein("xf", [NKH, 128, T], F32)
    w_in6 = ein("w_in6", [NMP, 128, NKH, 128], F32R)
    w_x6 = ein("w_x6", [NDC, 128, 288], F32R)
    w_dt6 = ein("w_dt6", [NDC, 128, 2, 128], F32R)
    w_out6 = ein("w_out6", [NMO, 128, NDC, 128], F32)
    wffn6 = ein("wffn6", [NMO, 128, NKH, 128], F32)
    lora_at = ein("lora_at", [128, 3 * NKH * NST], F32)
    lora_bt = ein("lora_bt", [16, 3 * H], F32)
    alog_t = ein("alog_t", [128, NDC * NST], F32)
    d_t = ein("d_t", [128, NDC], F32)
    convw_t = ein("convw_t", [128, NDC * KCV], F32)
    convb_t = ein("convb_t", [128, NDC], F32)
    bdt_t = ein("bdt_t", [128, NDC], F32)
    ffnb_t = ein("ffnb_t", [128, NMO], F32)
    lsi = ein("lsi", [16, 4], I32)
    cid = ein("cid", [16, 1], F32)
    ident = ein("ident", [128, 128], F32R)
    ind16 = ein("ind16", [16, NST * 128], F32R)
    onescol = ein("onescol", [128, 1], F32R)
    consts = ein("consts", [128, 4], F32)
    onesrow = ein("onesrow", [1, 128], F32R)

    xn_r = scr("xn_r", [NKH, 128, T], F32R)
    xn_b = scr("xn_b", [NKH, 128, T], BF16)
    xcv = scr("xcv", [NDC, 128, T], F32)
    sgd = scr("sgd", [NDC, 128, T], F32)
    dts = scr("dts", [NDC, 128, T], F32)
    apd = scr("apd", [NDC, 128, T], BF16)
    dbg = scr("dbg", [6, 128, T], F32)
    out = nc.dram_tensor("out", [NMO, 128, T], F32, kind="ExternalOutput").ap()

    y_ps = nc.alloc_psum_tensor("y_ps", [128, 2048], F32).ap()
    pC = nc.alloc_psum_tensor("pC", [128, 1024], F32).ap()
    pC0 = nc.alloc_psum_tensor("pC0", [128, 1024], F32).ap()
    pA = pC0[:, 0:512]
    pB = pC0[:, 512:1024]

    cap = 206 * 1024
    SB_BASE = nc.bump_sbuf(cap)[0]
    off = [0]

    def mk(name, shape, dt, base):
        at = base[0]
        sz = int(np.prod(shape[1:])) * mybir.dt.size(dt)
        base[0] = (at + sz + 63) & ~63
        assert base[0] <= cap, f"sbuf overflow {name}: {base[0]}"
        return nc.alloc_sbuf_tensor_at(name, shape, dt,
                                       offset=SB_BASE + at).ap()

    sbP = lambda n, s, d: mk(n, s, d, off)

    s_ident = sbP("s_ident", [128, 128], F32R)
    s_onescol = sbP("s_onescol", [128, 1], F32R)
    s_const = sbP("s_const", [128, 4], F32)
    s_onesrow = sbP("s_onesrow", [1, 128], F32R)
    s_d = sbP("s_d", [128, NDC], F32)
    s_convw = sbP("s_convw", [128, NDC * KCV], F32)
    s_convb = sbP("s_convb", [128, NDC], F32)
    s_bdt = sbP("s_bdt", [128, NDC], F32)
    s_ffnb = sbP("s_ffnb", [128, NMO], F32)
    s_lsi = sbP("s_lsi", [16, 4], I32)
    s_lsif = sbP("s_lsif", [16, 4], F32)
    s_cid = sbP("s_cid", [16, 1], F32)
    s_mw = sbP("s_mw", [16, 8], F32)
    s_masks = sbP("s_masks", [16, 4], F32)
    s_Bn = sbP("s_Bn", [16, T], F32R)
    s_Cn = sbP("s_Cn", [16, T], F32R)
    P_END = off[0]
    assert P_END < 64 * 1024, P_END
    SCR = (P_END + 63) & ~63
    tail = [cap - 16 * 1024]
    s_ind16 = mk("s_ind16", [16, NST * 128], F32R, tail)
    s_alog = mk("s_alog", [128, NDC * NST], F32, tail)
    s_A = mk("s_A", [128, NDC * NST], F32, tail)
    TAIL0 = cap - 16 * 1024

    ctx = Ctx(nc)
    LD, GL, SS, PE, DV, AC = "LD", "GL", "SS", "PE", "DV", "AC"
    for s in (LD, GL, SS, PE, DV, AC):
        ctx.sem(s)

    def act(emit, waits=(), inc=True):
        ctx.add("scalar", emit, waits, [(AC, 1)] if inc else ())
        return (AC, ctx.val(AC))

    def dve(emit, waits=(), inc=True):
        ctx.add("vector", emit, waits, [(DV, 1)] if inc else ())
        return (DV, ctx.val(DV))

    def pe(emit, waits=(), inc=True):
        ctx.add("tensor", emit, waits, [(PE, 1)] if inc else ())
        return (PE, ctx.val(PE))

    for dst, src in [(s_ident, ident), (s_onescol, onescol),
                     (s_onesrow, onesrow), (s_d, d_t),
                     (s_convw, convw_t), (s_convb, convb_t), (s_bdt, bdt_t),
                     (s_ffnb, ffnb_t), (s_lsi, lsi), (s_cid, cid),
                     (s_const, consts)]:
        t_pre = ctx.dma("sync", "PRE", dst[:], src[:])

    PL = "PL"
    ctx.sem(PL)
    ctx.add("gpsimd", lambda e: e.tensor_copy(out=s_lsif[:], in_=s_lsi[:]),
            [t_pre], [(PL, 1)])
    t_pl = (PL, ctx.val(PL))
    tm = dve(lambda e: e.tensor_single_scalar(out=s_mw[:, 0:4], in_=s_lsif[:],
                                              scalar=s_cid[:], op=ALU.is_le),
             [t_pl, t_pre])
    ts_ = act(lambda e: e.activation(out=s_mw[:, 0:4], in_=s_mw[:, 0:4],
                                     func=AF.Copy,
                                     accum_out=s_mw[:, 4:5]), [tm])
    dve(lambda e: e.tensor_single_scalar(out=s_masks[:, 0:1],
                                         in_=s_mw[:, 4:5], scalar=2.0,
                                         op=ALU.is_equal), [ts_], inc=False)
    dve(lambda e: e.tensor_single_scalar(out=s_masks[:, 1:2],
                                         in_=s_mw[:, 4:5], scalar=3.0,
                                         op=ALU.is_equal), inc=False)
    dve(lambda e: e.tensor_single_scalar(out=s_masks[:, 2:3],
                                         in_=s_mw[:, 4:5], scalar=4.0,
                                         op=ALU.is_ge))
    t_masks = (DV, ctx.val(DV))
    if _DEBUG:
        ctx.dma("scalar", SS, dbg[0, 0:16, 0:4], s_masks[:], [t_masks])
        ctx.dma("scalar", SS, dbg[0, 16:32, 0:8], s_mw[:], [t_masks])

    o0 = [SCR]
    sb0 = lambda n, s, d: mk(n, s, d, o0)
    x0 = [sb0("st0_x0", [128, T], F32R), sb0("st0_x1", [128, T], F32R)]
    xf0 = [sb0("st0_xf0", [128, T], F32), sb0("st0_xf1", [128, T], F32)]
    sq0 = sb0("st0_sq", [128, T], F32R)
    sxs = sb0("st0_sxs", [1, T], F32)
    sqs = sb0("st0_sqs", [1, T], F32)
    nm0 = sb0("st0_nm", [1, T], F32)
    var0 = sb0("st0_var", [1, T], F32)
    rstd_r = sb0("st0_rstd", [1, T], F32R)
    nmr_r = sb0("st0_nmr", [1, T], F32R)
    rrep = sb0("st0_rrep", [128, T], F32)
    mrep = sb0("st0_mrep", [128, T], F32)
    xt0 = sb0("st0_xt", [128, T], F32)
    xnr0 = [sb0("st0_xnr0", [128, T], F32R), sb0("st0_xnr1", [128, T], F32R)]

    stats_banks = [pA[0:1, 0:512], pB[0:1, 0:512], pC[0:1, 0:512],
                   pC[0:1, 512:1024]]
    sq_done = {}
    mm_done = {}
    for kc in range(NKH):
        sl = kc % 2
        tld = ctx.dma("sync", f"X{sl}", x0[sl][:], xr[kc],
                      [sq_done.get(kc - 2), mm_done.get(kc - 2)])
        sq_done[kc] = act(lambda e, sl=sl: e.activation(
            out=sq0[:], in_=x0[sl][:], func=AF.Square),
            [tld, sq_done.get(kc - 1), mm_done.get(kc - 1)])
        for tt in range(4):
            pe(lambda e, sl=sl, tt=tt, kc=kc: e.matmul(
                stats_banks[tt], s_onescol[:],
                x0[sl][:, tt * 512:(tt + 1) * 512],
                start=(kc == 0), stop=(kc == NKH - 1)),
                [tld] if tt == 0 else (), inc=False)
        for tt in range(4):
            pe(lambda e, tt=tt, kc=kc: e.matmul(
                y_ps[0:1, tt * 512:(tt + 1) * 512], s_onescol[:],
                sq0[:, tt * 512:(tt + 1) * 512],
                start=(kc == 0), stop=(kc == NKH - 1)),
                [sq_done[kc]] if tt == 0 else (), inc=(tt == 3))
        mm_done[kc] = (PE, ctx.val(PE))
    t_stats_mm = mm_done[NKH - 1]
    for tt in range(4):
        act(lambda e, tt=tt: e.activation(
            out=sxs[:, tt * 512:(tt + 1) * 512], in_=stats_banks[tt],
            func=AF.Copy), [t_stats_mm] if tt == 0 else ())
    for tt in range(4):
        t_stats_ev = act(lambda e, tt=tt: e.activation(
            out=sqs[:, tt * 512:(tt + 1) * 512],
            in_=y_ps[0:1, tt * 512:(tt + 1) * 512], func=AF.Copy))
    dve(lambda e: e.tensor_scalar(out=nm0[:], in0=sxs[:], scalar1=-1.0 / H,
                                  scalar2=None, op0=ALU.mult), [t_stats_ev],
        inc=False)
    dve(lambda e: e.tensor_scalar(out=var0[:], in0=sqs[:], scalar1=1.0 / H,
                                  scalar2=None, op0=ALU.mult), inc=False)
    dve(lambda e: e.tensor_mul(out=sxs[:], in0=nm0[:], in1=nm0[:]), inc=False)
    t_var = dve(lambda e: e.tensor_sub(out=var0[:], in0=var0[:], in1=sxs[:]))
    act(lambda e: e.activation(out=var0[:], in_=var0[:], func=AF.Ln,
                               bias=s_const[0:1, 0:1]), [t_var], inc=False)
    t_rstd = act(lambda e: e.activation(out=rstd_r[:], in_=var0[:],
                                        func=AF.Exp, scale=-0.5))
    t_nmr = dve(lambda e: e.tensor_mul(out=nmr_r[:], in0=nm0[:],
                                       in1=rstd_r[:].bitcast(F32)), [t_rstd])
    last_ev = t_stats_ev
    for tt in range(4):
        tb1 = pe(lambda e, tt=tt: e.matmul(
            pA[:, 0:512], s_onesrow[:], rstd_r[:, tt * 512:(tt + 1) * 512],
            start=True, stop=True), [t_rstd, last_ev])
        e1 = act(lambda e, tt=tt: e.activation(
            out=rrep[:, tt * 512:(tt + 1) * 512], in_=pA[:, 0:512],
            func=AF.Copy), [tb1])
        tb2 = pe(lambda e, tt=tt: e.matmul(
            pA[:, 0:512], s_onesrow[:], nmr_r[:, tt * 512:(tt + 1) * 512],
            start=True, stop=True), [t_nmr, e1])
        last_ev = act(lambda e, tt=tt: e.activation(
            out=mrep[:, tt * 512:(tt + 1) * 512], in_=pA[:, 0:512],
            func=AF.Copy), [tb2])
    t_reps = last_ev
    st0_ss = {}
    st0_gl = {}
    st0_dv = {}
    for kc in range(NKH):
        sl = kc % 2
        tld = ctx.dma("sync", f"F{sl}", xf0[sl][:], xf[kc],
                      [st0_dv.get(kc - 2)])
        dve(lambda e, sl=sl: e.tensor_mul(out=xt0[:], in0=xf0[sl][:],
                                          in1=rrep[:]),
            [tld, t_reps], inc=False)
        st0_dv[kc] = dve(lambda e, sl=sl: e.tensor_add(
            out=xnr0[sl][:], in0=xt0[:], in1=mrep[:]),
            [st0_ss.get(kc - 2), st0_gl.get(kc - 2)])
        ctx.dma("scalar", f"XR{sl}", xn_r[kc], xnr0[sl][:], [st0_dv[kc]])
        st0_ss[kc] = st0_gl[kc] = ctx.dma(
            "gpsimd", f"XR{sl}", xn_b[kc],
            xnr0[sl][:].bitcast(F32), [st0_dv[kc]])

    if max_stage >= 1:
        o1 = [SCR]
        sb1 = lambda n, s, d: mk(n, s, d, o1)
        xh1 = sb1("st1_xh", [128, NKH * 1024], F32R)
        w1 = [sb1(f"st1_w{i}", [128, 2048], F32R) for i in range(4)]
        ext1 = [sb1(f"st1_ext{i}", [128, 516], F32) for i in range(2)]
        cc1 = [sb1(f"st1_cc{i}", [128, 512], F32) for i in range(2)]
        xv1 = [sb1(f"st1_xv{i}", [128, 512], F32) for i in range(2)]
        sg1 = [sb1(f"st1_sg{i}", [128, 512], F32) for i in range(2)]
        tails = sb1("st1_tails", [128, NDC * 4], F32)

        t_tail0 = dve(lambda e: e.memset(tails[:], 0.0))

        ldx1 = {}
        whalf_mm = {}
        grp_mm = {}
        ev_done = {}
        xv_ss = {}
        sg_ss = {}
        last_h_mm = None
        for h in range(2):
            st0_drain = [("XR0", ctx.val("XR0")), ("XR1", ctx.val("XR1")),
                         st0_dv[NKH - 1]]
            for kc in range(NKH):
                ldx1[(h, kc)] = ctx.dma(
                    "sync", "XH", xh1[:, kc * 1024:(kc + 1) * 1024],
                    xn_r[kc, :, h * 1024:(h + 1) * 1024].opt(),
                    (st0_drain + [last_h_mm]) if kc == 0 else ())
            for wda in range(4):
                pe(lambda e: e.matmul(pC[:, 0:128], s_ident[:], s_ident[:],
                                      start=True, stop=True),
                   [ldx1[(h, 0)]] if wda == 0 else (), inc=(wda == 3))
            for mc in range(NMP):
                g = h * NMP + mc
                ldw = {}
                gpar = g % 2
                for wh in range(2):
                    ldw[wh] = ctx.dma(
                        "sync", f"W{gpar}{wh}", w1[gpar * 2 + wh][:],
                        w_in6[mc, :, wh * 16:(wh + 1) * 16, :].opt(),
                        [whalf_mm.get((g - 2, wh))])
                for kci in range(NKH):
                    wh = kci // 16
                    wt = w1[gpar * 2 + wh]
                    for tt in range(2):
                        ps = pA if tt == 0 else pB
                        waits = []
                        if kci == 0 and tt == 0:
                            waits = [ldw[0], ldx1[(h, NKH - 1)],
                                     ev_done.get((g - 1, 0)),
                                     ev_done.get((g - 1, 1))]
                        elif kci == 16 and tt == 0:
                            waits = [ldw[1]]
                        pe(lambda e, ps=ps, wt=wt, kci=kci, tt=tt: e.matmul(
                            ps[:, 0:512],
                            wt[:, (kci % 16) * 128:((kci % 16) + 1) * 128],
                            xh1[:, kci * 1024 + tt * 512:kci * 1024 + tt * 512 + 512],
                            start=(kci == 0), stop=(kci == NKH - 1)),
                            waits, inc=(kci == NKH - 1 or (kci == 15 and tt == 1)))
                    if kci == 15:
                        whalf_mm[(g, 0)] = (PE, ctx.val(PE))
                grp_mm[(g, 0)] = grp_mm[(g, 1)] = whalf_mm[(g, 1)] = \
                    (PE, ctx.val(PE))
                for tt in range(2):
                    ps = pA if tt == 0 else pB
                    tmm = grp_mm[(g, tt)]
                    gt = 2 * h + tt
                    es = gt % 2
                    if mc < NDC:
                        dc = mc
                        tac1 = dve(lambda e, ps=ps, es=es: e.tensor_copy(
                            out=ext1[es][:, 4:516], in_=ps[:, 0:512]), [tmm])
                        ev_done[(g, tt)] = tac1
                        dve(lambda e, es=es, dc=dc: e.tensor_copy(
                            out=ext1[es][:, 1:4],
                            in_=tails[:, dc * 4 + 1:dc * 4 + 4]),
                            [t_tail0], inc=False)
                        dve(lambda e, es=es, dc=dc: e.tensor_copy(
                            out=tails[:, dc * 4 + 1:dc * 4 + 4],
                            in_=ext1[es][:, 513:516]), inc=False)
                        dve(lambda e, es=es, dc=dc: e.tensor_scalar(
                            out=cc1[es][:], in0=ext1[es][:, 1:513],
                            scalar1=s_convw[:, dc * 4:dc * 4 + 1],
                            scalar2=s_convb[:, dc:dc + 1],
                            op0=ALU.mult, op1=ALU.add), inc=False)
                        for j in range(1, 4):
                            dve(lambda e, es=es, dc=dc, j=j:
                                e.scalar_tensor_tensor(
                                    out=cc1[es][:], in0=ext1[es][:, 1 + j:513 + j],
                                    scalar=s_convw[:, dc * 4 + j:dc * 4 + j + 1],
                                    in1=cc1[es][:], op0=ALU.mult, op1=ALU.add),
                                inc=(j == 3))
                        tcv = (DV, ctx.val(DV))
                        tsil = act(lambda e, es=es: e.activation(
                            out=xv1[es][:], in_=cc1[es][:], func=AF.Silu),
                            [tcv, xv_ss.get(es)])
                        xv_ss[es] = ctx.dma(
                            "scalar", f"XV{es}",
                            xcv[dc, :, gt * 512:(gt + 1) * 512].opt(), xv1[es][:],
                            [tsil])
                    else:
                        gc = mc - NDC
                        tev = act(lambda e, ps=ps, es=es: e.activation(
                            out=sg1[es][:], in_=ps[:, 0:512], func=AF.Silu),
                            [tmm, sg_ss.get(es)])
                        ev_done[(g, tt)] = tev
                        sg_ss[es] = ctx.dma(
                            "scalar", f"SG{es}",
                            sgd[gc, :, gt * 512:(gt + 1) * 512].opt(), sg1[es][:],
                            [tev])
            last_h_mm = grp_mm[(h * NMP + NMP - 1, 1)]
        t_st1_ss = (SS, ctx.val(SS))

    if max_stage >= 2:
        o2 = [SCR]
        sb2 = lambda n, s, d: mk(n, s, d, o2)
        xcr2 = [sb2(f"st2_xcr{i}", [128, 512], F32R) for i in range(2)]
        wx2 = [sb2(f"st2_wx{i}", [128, 288], F32R) for i in range(2)]
        ssm0 = sb2("st2_ssm0", [128, T], F32)
        ssm1 = sb2("st2_ssm1", [128, T], F32)
        ssm2 = sb2("st2_ssm2", [16, 2 * T], F32)
        sqr2 = sb2("st2_sqr", [128, T], F32R)
        tsrep = sb2("st2_tsrep", [128, T], F32)
        tsn = [sb2(f"st2_tsn{i}", [128, T], F32R) for i in range(2)]
        strow = sb2("st2_strow", [1, T], F32)
        rstd2 = sb2("st2_rstd2", [1, T], F32R)
        srowB = sb2("st2_srowB", [1, T], F32)
        srowC = sb2("st2_srowC", [1, T], F32)
        rstdB = sb2("st2_rstdB", [1, T], F32R)
        rstdC = sb2("st2_rstdC", [1, T], F32R)
        repB = sb2("st2_repB", [16, T], F32)
        repC = sb2("st2_repC", [16, T], F32)
        wdt2 = [sb2(f"st2_wdt{i}", [128, 256], F32R) for i in range(2)]
        sp_t = [sb2(f"st2_spt{i}", [128, 512], F32) for i in range(2)]
        sp_m = [sb2(f"st2_spm{i}", [128, 512], F32) for i in range(2)]
        dt_t = [sb2(f"st2_dt{i}", [128, 512], F32) for i in range(2)]

        wtail = [("XV0", ctx.val("XV0")), ("XV1", ctx.val("XV1")),
                 ("SG0", ctx.val("SG0")), ("SG1", ctx.val("SG1"))]
        ctx.dma("sync", "PRE", s_ind16[:], ind16[:], wtail)
        t_tailld = ctx.dma("sync", "PRE", s_alog[:], alog_t[:], ())
        mm_1b = {}
        ev_1b = None
        for tt in range(4):
            for dc in range(NDC):
                sl = dc % 2
                g = tt * NDC + dc
                lw = [mm_1b.get(g - 2)]
                if g == 0:
                    lw += [("XV0", ctx.val("XV0")), ("XV1", ctx.val("XV1")),
                           ("SG0", ctx.val("SG0")), ("SG1", ctx.val("SG1"))]
                tx = ctx.dma("sync", f"XC{sl}", xcr2[sl][:],
                             xcv[dc, :, tt * 512:(tt + 1) * 512]
                             .bitcast(F32R).opt(), lw)
                tw = ctx.dma("sync", f"WX{sl}", wx2[sl][:], w_x6[dc], ())
                pewaits = [tx, tw]
                if dc == 0 and tt > 0:
                    pewaits.append(ev_1b)
                pe(lambda e, sl=sl, dc=dc: e.matmul(
                    pA[:, 0:512], wx2[sl][:, 0:128], xcr2[sl][:],
                    start=(dc == 0), stop=(dc == NDC - 1)), pewaits, inc=False)
                pe(lambda e, sl=sl, dc=dc: e.matmul(
                    pB[:, 0:512], wx2[sl][:, 128:256], xcr2[sl][:],
                    start=(dc == 0), stop=(dc == NDC - 1)), inc=False)
                pe(lambda e, sl=sl, dc=dc: e.matmul(
                    pC[0:16, 0:512], wx2[sl][:, 256:272], xcr2[sl][:],
                    start=(dc == 0), stop=(dc == NDC - 1)), inc=False)
                pe(lambda e, sl=sl, dc=dc: e.matmul(
                    pC[0:16, 512:1024], wx2[sl][:, 272:288], xcr2[sl][:],
                    start=(dc == 0), stop=(dc == NDC - 1)), inc=True)
                mm_1b[g] = (PE, ctx.val(PE))
            tmm = mm_1b[tt * NDC + NDC - 1]
            act(lambda e, tt=tt: e.activation(
                out=ssm0[:, tt * 512:(tt + 1) * 512], in_=pA[:, 0:512],
                func=AF.Copy), [tmm], inc=False)
            act(lambda e, tt=tt: e.activation(
                out=ssm1[:, tt * 512:(tt + 1) * 512], in_=pB[:, 0:512],
                func=AF.Copy), inc=False)
            act(lambda e, tt=tt: e.activation(
                out=ssm2[:, tt * 512:(tt + 1) * 512], in_=pC[0:16, 0:512],
                func=AF.Copy), inc=False)
            ev_1b = act(lambda e, tt=tt: e.activation(
                out=ssm2[:, T + tt * 512:T + (tt + 1) * 512],
                in_=pC[0:16, 512:1024], func=AF.Copy))
        t_ssm = ev_1b

        SUB2 = sub2
        if SUB2 >= 1:
            tsq = None
            t_mm_prev = None
            for kci, smt in enumerate((ssm0, ssm1)):
                tsq = act(lambda e, smt=smt: e.activation(
                    out=sqr2[:], in_=smt[:], func=AF.Square),
                    [t_ssm] if kci == 0 else [t_mm_prev])
                for tt in range(4):
                    pe(lambda e, tt=tt, kci=kci: e.matmul(
                        stats_banks[tt], s_onescol[:],
                        sqr2[:, tt * 512:(tt + 1) * 512],
                        start=(kci == 0), stop=(kci == 1)),
                        [tsq] if tt == 0 else (), inc=(tt == 3))
                t_mm_prev = (PE, ctx.val(PE))
            t_tsmm = (PE, ctx.val(PE))
            for tt in range(4):
                t_tsev = act(lambda e, tt=tt: e.activation(
                    out=strow[:, tt * 512:(tt + 1) * 512], in_=stats_banks[tt],
                    func=AF.Copy), [t_tsmm] if tt == 0 else ())
            act(lambda e: e.activation(out=strow[:], in_=strow[:], func=AF.Ln,
                                       bias=s_const[0:1, 1:2], scale=1.0 / R),
                [t_tsev], inc=False)
            t_rstd2 = act(lambda e: e.activation(out=rstd2[:], in_=strow[:],
                                                 func=AF.Exp, scale=-0.5))
            last = t_rstd2
            for tt in range(4):
                tb1 = pe(lambda e, tt=tt: e.matmul(
                    pA[:, 0:512], s_onesrow[:], rstd2[:, tt * 512:(tt + 1) * 512],
                    start=True, stop=True), [t_rstd2, last])
                last = dve(lambda e, tt=tt: e.tensor_copy(
                    out=tsrep[:, tt * 512:(tt + 1) * 512], in_=pA[:, 0:512]),
                    [tb1])
            t_tsrep = last
            dve(lambda e: e.tensor_mul(out=tsn[0][:], in0=ssm0[:], in1=tsrep[:]),
                [t_tsrep], inc=False)
            t_tsn = dve(lambda e: e.tensor_mul(out=tsn[1][:], in0=ssm1[:],
                                               in1=tsrep[:]))

        if SUB2 >= 2:
            tBsq = act(lambda e: e.activation(out=sqr2[0:16, :], in_=ssm2[:, 0:T],
                                              func=AF.Square), [t_tsrep])
            for tt in range(4):
                pe(lambda e, tt=tt: e.matmul(
                    stats_banks[tt], s_onescol[0:16, :],
                    sqr2[0:16, tt * 512:(tt + 1) * 512], start=True, stop=True),
                    [tBsq] if tt == 0 else (), inc=(tt == 3))
            tBmm = (PE, ctx.val(PE))
            for tt in range(4):
                tBev = act(lambda e, tt=tt: e.activation(
                    out=srowB[:, tt * 512:(tt + 1) * 512], in_=stats_banks[tt],
                    func=AF.Copy), [tBmm] if tt == 0 else ())
            tCsq = act(lambda e: e.activation(out=sqr2[0:16, :], in_=ssm2[:, T:2 * T],
                                              func=AF.Square), [tBev])
            for tt in range(4):
                pe(lambda e, tt=tt: e.matmul(
                    stats_banks[tt], s_onescol[0:16, :],
                    sqr2[0:16, tt * 512:(tt + 1) * 512], start=True, stop=True),
                    [tCsq] if tt == 0 else (), inc=(tt == 3))
            tCmm = (PE, ctx.val(PE))
            for tt in range(4):
                tCev = act(lambda e, tt=tt: e.activation(
                    out=srowC[:, tt * 512:(tt + 1) * 512], in_=stats_banks[tt],
                    func=AF.Copy), [tCmm] if tt == 0 else ())
            act(lambda e: e.activation(out=srowB[:], in_=srowB[:], func=AF.Ln,
                                       bias=s_const[0:1, 1:2], scale=1.0 / NST),
                [tCev], inc=False)
            act(lambda e: e.activation(out=rstdB[:], in_=srowB[:], func=AF.Exp,
                                       scale=-0.5), inc=False)
            act(lambda e: e.activation(out=srowC[:], in_=srowC[:], func=AF.Ln,
                                       bias=s_const[0:1, 1:2], scale=1.0 / NST),
                inc=False)
            t_rstdBC = act(lambda e: e.activation(out=rstdC[:], in_=srowC[:],
                                                  func=AF.Exp, scale=-0.5))
            last = t_rstdBC
            for tt in range(4):
                tb1 = pe(lambda e, tt=tt: e.matmul(
                    pA[0:16, 0:512], s_onesrow[:, 0:16],
                    rstdB[:, tt * 512:(tt + 1) * 512], start=True, stop=True),
                    [t_rstdBC, last])
                eB = dve(lambda e, tt=tt: e.tensor_copy(
                    out=repB[:, tt * 512:(tt + 1) * 512], in_=pA[0:16, 0:512]),
                    [tb1])
                tb2 = pe(lambda e, tt=tt: e.matmul(
                    pB[0:16, 0:512], s_onesrow[:, 0:16],
                    rstdC[:, tt * 512:(tt + 1) * 512], start=True, stop=True), [eB])
                last = dve(lambda e, tt=tt: e.tensor_copy(
                    out=repC[:, tt * 512:(tt + 1) * 512], in_=pB[0:16, 0:512]),
                    [tb2])
            t_repBC = last
            dve(lambda e: e.tensor_mul(out=s_Bn[:], in0=ssm2[:, 0:T], in1=repB[:]),
                [t_repBC], inc=False)
            t_BCn = dve(lambda e: e.tensor_mul(out=s_Cn[:], in0=ssm2[:, T:2 * T],
                                               in1=repC[:]))
            if _DEBUG and _TAPS:
                ctx.dma("scalar", SS, dbg[0], ssm0[:], [t_BCn])
                ctx.dma("scalar", SS, dbg[1], ssm1[:], [t_BCn])
                ctx.dma("scalar", SS, dbg[2], tsrep[:], [t_BCn])
                ctx.dma("scalar", SS, dbg[3], tsn[0][:].bitcast(F32), [t_BCn])
                ctx.dma("scalar", SS, dbg[4], tsn[1][:].bitcast(F32), [t_BCn])
                ctx.dma("scalar", SS, dbg[5, 0:1, :], strow[:], [t_BCn])
                ctx.dma("scalar", SS, dbg[5, 1:2, :], rstd2[:].bitcast(F32),
                        [t_BCn])
                ctx.dma("scalar", SS, dbg[5, 2:3, :], s_Bn[0:1, :].bitcast(F32),
                        [t_BCn])

        if SUB2 >= 3:
            wdt_mm = {}
            dt_ps = {}
            dt_ss = {}
            for mc in range(NDC):
                sl = mc % 2
                twd = ctx.dma("sync", f"WD{sl}", wdt2[sl][:],
                              w_dt6[mc].opt(), [wdt_mm.get(mc - 2)])
                for tt in range(4):
                    gseq = mc * 4 + tt
                    ps = pA if tt % 2 == 0 else pB
                    prev = dt_ps.get(gseq - 2)
                    waits = [twd, t_tsn, t_repBC]
                    if prev is not None:
                        waits += [prev[0], prev[1]]
                    for kci in range(2):
                        pe(lambda e, sl=sl, kci=kci, tt=tt, ps=ps: e.matmul(
                            ps[:, 0:512], wdt2[sl][:, kci * 128:(kci + 1) * 128],
                            tsn[kci][:, tt * 512:(tt + 1) * 512],
                            start=(kci == 0), stop=(kci == 1)),
                            waits if kci == 0 else (), inc=(kci == 1))
                    tmm = (PE, ctx.val(PE))
                    wdt_mm[mc] = tmm
                    es = tt % 2
                    wprev = [tmm, dt_ps.get(gseq - 2, ((DV, 0),))[-1]]
                    if sub2 >= 5:
                        a2 = act(lambda e, ps=ps, es=es: e.activation(
                            out=dt_t[es][:], in_=ps[:, 0:512], func=AF.Copy),
                            [tmm, dt_ss.get(gseq - 2)])
                        dt_ps[gseq] = (a2, a2)
                        dt_ss[gseq] = ctx.dma(
                            "scalar", f"DT{es}",
                            dts[mc, :, tt * 512:(tt + 1) * 512].opt(),
                            dt_t[es][:], [a2])
                        continue
                    if sub2 >= 4:
                        a1 = act(lambda e, ps=ps, es=es: e.activation(
                            out=sp_t[es][:], in_=ps[:, 0:512], func=AF.Square),
                            wprev)
                        a2 = a1
                    else:
                        a1 = act(lambda e, ps=ps, es=es, mc=mc: e.activation(
                            out=sp_t[es][:], in_=ps[:, 0:512], func=AF.Abs,
                            bias=s_bdt[:, mc:mc + 1]), wprev)
                        act(lambda e, es=es: e.activation(
                            out=sp_t[es][:], in_=sp_t[es][:], func=AF.Exp,
                            scale=-1.0), inc=False)
                        a2 = act(lambda e, es=es: e.activation(
                            out=sp_t[es][:], in_=sp_t[es][:], func=AF.Ln,
                            bias=s_const[:, 2:3]))
                    dve(lambda e, ps=ps, es=es, mc=mc: e.tensor_scalar(
                        out=sp_m[es][:], in0=ps[:, 0:512],
                        scalar1=s_bdt[:, mc:mc + 1], scalar2=0.0,
                        op0=ALU.add, op1=ALU.max), [a1], inc=False)
                    d1 = dve(lambda e, es=es: e.tensor_add(
                        out=dt_t[es][:], in0=sp_m[es][:], in1=sp_t[es][:]),
                        [a2, dt_ss.get(gseq - 2)])
                    dt_ps[gseq] = (a2, d1)
                    dt_ss[gseq] = ctx.dma(
                        "scalar", f"DT{es}",
                        dts[mc, :, tt * 512:(tt + 1) * 512].opt(),
                        dt_t[es][:], [d1])
        t_st2_alldv = (DV, ctx.val(DV))
        t_st2_done = [("DT0", ctx.val("DT0")), ("DT1", ctx.val("DT1")),
                      ("XV0", ctx.val("XV0")), ("XV1", ctx.val("XV1")),
                      ("SG0", ctx.val("SG0")), ("SG1", ctx.val("SG1"))]

    if max_stage >= 3:
        o3 = [SCR]
        sb3 = lambda n, s, d: mk(n, s, d, o3)
        dt3 = [sb3(f"st3_dt{i}", [128, T], F32) for i in range(2)]
        u3 = [sb3(f"st3_u{i}", [128, T], F32) for i in range(2)]
        sg3 = [sb3(f"st3_sg{i}", [128, T], F32) for i in range(2)]
        db3 = [sb3(f"st3_db{i}", [128, T], F32) for i in range(2)]
        dA3 = [sb3(f"st3_dA{i}", [128, T], F32) for i in range(2)]
        bb3 = sb3("st3_bb", [128, T], F32)
        stt3 = sb3("st3_stt", [128, T], F32)
        pr3 = [sb3(f"st3_pr{i}", [128, 1024], F32R) for i in range(2)]
        e13 = sb3("st3_e1", [128, T], F32)
        ap3 = [sb3(f"st3_ap{i}", [128, T], BF16) for i in range(2)]

        tA1 = act(lambda e: e.activation(out=s_A[:], in_=s_alog[:],
                                         func=AF.Exp), [t_tailld])
        tA2 = dve(lambda e: e.tensor_scalar(out=s_A[:], in0=s_A[:], scalar1=-1.0,
                                            scalar2=None, op0=ALU.mult), [tA1])

        scn = {}
        dcdone = {}
        ap_ss = {}
        t_pr_free = [t_BCn, t_BCn]
        t_yps_free = t_BCn
        pr_pe = [(PE, 0), (PE, 0)]
        t_pC_free = t_BCn
        for dc in range(NDC):
            sl = dc % 2
            lw = [dcdone.get(dc - 2)]
            if dc == 0:
                lw += t_st2_done
            ctx.dma("sync", f"SC{sl}", dt3[sl][:], dts[dc], lw)
            ctx.dma("sync", f"SC{sl}", u3[sl][:], xcv[dc], ())
            l3 = l1 = ctx.dma("sync", f"SC{sl}", sg3[sl][:], sgd[dc], ())
            tdb = dve(lambda e, sl=sl: e.tensor_mul(
                out=db3[sl][:], in0=dt3[sl][:], in1=u3[sl][:]),
                [l1, tA2, t_BCn])
            for n in range(NST):
                asl = n % 2
                ind_n = s_ind16[:, n * 128:(n + 1) * 128]
                wk0 = [cons0, cons1, t_BCn] if (dc, n) != (0, 0) else \
                    [t_BCn, t_st2_alldv, t_yps_free, t_tailld]
                pe(lambda e, n=n, ind_n=ind_n: e.matmul(
                    pC[:, 0:512], ind_n, s_Bn[:, 0:512],
                    start=True, stop=True), wk0, inc=False)
                tk0 = pe(lambda e, ind_n=ind_n: e.matmul(
                    pC[:, 512:1024], ind_n, s_Bn[:, 512:1024],
                    start=True, stop=True))
                wda = [l1, tA2]
                if n >= 2:
                    wda.append(scn[(dc, n - 2)])
                elif dc >= 1:
                    wda.append(dcdone.get(dc - 1))
                tda = act(lambda e, sl=sl, asl=asl, dc=dc, n=n: e.activation(
                    out=dA3[asl][:], in_=dt3[sl][:], func=AF.Exp,
                    scale=s_A[:, dc * NST + n:dc * NST + n + 1]), wda)
                tbb0 = dve(lambda e, sl=sl: e.tensor_mul(
                    out=bb3[:, 0:1024], in0=db3[sl][:, 0:1024],
                    in1=pC[:, 0:1024]), [tk0, tdb])
                wk1 = [cons1, tbb0] if (dc, n) != (0, 0) else \
                    [t_st2_alldv, tbb0]
                pe(lambda e, ind_n=ind_n: e.matmul(
                    pC0[:, 0:512], ind_n, s_Bn[:, 1024:1536],
                    start=True, stop=True), wk1, inc=False)
                tk1 = pe(lambda e, ind_n=ind_n: e.matmul(
                    pC0[:, 512:1024], ind_n, s_Bn[:, 1536:2048],
                    start=True, stop=True))
                tbb1 = dve(lambda e, sl=sl: e.tensor_mul(
                    out=bb3[:, 1024:2048], in0=db3[sl][:, 1024:2048],
                    in1=pC0[:, 0:1024]), [tk1])
                tscan = dve(lambda e, asl=asl: e.tensor_tensor_scan(
                    out=stt3[:], data0=dA3[asl][:], data1=bb3[:], initial=0.0,
                    op0=ALU.mult, op1=ALU.add), [tda])
                scn[(dc, n)] = tscan
                pe(lambda e, ind_n=ind_n: e.matmul(
                    pC[:, 0:512], ind_n, s_Cn[:, 0:512],
                    start=True, stop=True), [tbb0], inc=False)
                tk2 = pe(lambda e, ind_n=ind_n: e.matmul(
                    pC[:, 512:1024], ind_n, s_Cn[:, 512:1024],
                    start=True, stop=True))
                pe(lambda e, ind_n=ind_n: e.matmul(
                    pC0[:, 0:512], ind_n, s_Cn[:, 1024:1536],
                    start=True, stop=True), [tbb1], inc=False)
                tk3 = pe(lambda e, ind_n=ind_n: e.matmul(
                    pC0[:, 512:1024], ind_n, s_Cn[:, 1536:2048],
                    start=True, stop=True))
                cons0 = dve(lambda e: e.tensor_mul(
                    out=pr3[0][:], in0=stt3[:, 0:1024], in1=pC[:, 0:1024]),
                    [tk2, pr_pe[0]] + ([t_yps_free] if n == 0 else []))
                for q in range(2):
                    pe(lambda e, q=q, n=n: e.matmul(
                        y_ps[:, q * 512:q * 512 + 512],
                        s_ident[:], pr3[0][:, q * 512:(q + 1) * 512],
                        start=(n == 0), stop=(n == NST - 1)),
                        [cons0] if q == 0 else (), inc=(q == 1))
                pr_pe[0] = (PE, ctx.val(PE))
                cons1 = dve(lambda e: e.tensor_mul(
                    out=pr3[1][:], in0=stt3[:, 1024:2048], in1=pC0[:, 0:1024]),
                    [tk3, pr_pe[1]])
                for q in range(2):
                    pe(lambda e, q=q, n=n: e.matmul(
                        y_ps[:, 1024 + q * 512:1024 + q * 512 + 512],
                        s_ident[:], pr3[1][:, q * 512:(q + 1) * 512],
                        start=(n == 0), stop=(n == NST - 1)),
                        [cons1] if q == 0 else (), inc=(q == 1))
                pr_pe[1] = (PE, ctx.val(PE))
            t_ylast = pr_pe[1]
            dve(lambda e, sl=sl, dc=dc: e.scalar_tensor_tensor(
                out=e13[:], in0=u3[sl][:], scalar=s_d[:, dc:dc + 1],
                in1=y_ps[:], op0=ALU.mult, op1=ALU.add),
                [t_ylast, l3], inc=False)
            t_yps_free = dve(lambda e, dc=dc, sl=sl: e.tensor_mul(
                out=ap3[dc % 2][:], in0=e13[:], in1=sg3[sl][:]),
                [ap_ss.get(dc - 2)])
            dcdone[dc] = t_yps_free
            ap_ss[dc] = ctx.dma("scalar", f"AP{dc % 2}", apd[dc],
                                ap3[dc % 2][:], [dcdone[dc]])
        t_st3_done = [("AP0", ctx.val("AP0")), ("AP1", ctx.val("AP1"))]

    if max_stage >= 4:
        o4 = [SCR]
        sb4 = lambda n, s, d: mk(n, s, d, o4)
        ap4 = sb4("st4_ap", [128, NDC * 512], BF16)
        xnb4 = sb4("st4_xnb", [128, NKH * 512], BF16)
        wout4 = [sb4(f"st4_wout{i}", [128, NDC * 128], BF16)
                 for i in range(2)]
        wffn4 = [sb4(f"st4_wffn{i}", [128, NKH * 128], BF16)
                 for i in range(2)]
        lA4 = sb4("st4_lA", [128, 3 * NKH * NST], BF16)
        lB4 = sb4("st4_lB", [16, 3 * H], BF16)
        skm4 = sb4("st4_skm", [16, 3 * 512], BF16)
        xf4 = [sb4(f"st4_xf{i}", [128, 512], F32) for i in range(2)]
        o14 = [sb4(f"st4_o1{i}", [128, 512], F32) for i in range(2)]
        ot4 = [sb4(f"st4_ot{i}", [128, 512], F32) for i in range(2)]

        t_st3_pe = (PE, ctx.val(PE))
        tlA = ctx.dma("gpsimd", GL, lA4[:], lora_at[:],
                      [t_yps_free, t_st3_pe])
        tlB = ctx.dma("gpsimd", GL, lB4[:], lora_bt[:], ())

        last_wmm = (PE, 0)
        wg_pe = {}
        sk_dv = None
        ev4 = {}
        out_ss = {}
        for tb in range(4):
            lap = None
            for dc in range(NDC):
                ws = ()
                if dc == 0:
                    ws = t_st3_done + [last_wmm]
                    if tb == 0:
                        ws += [("XR0", ctx.val("XR0")),
                               ("XR1", ctx.val("XR1"))]
                lap = ctx.dma("sync", "L4", ap4[:, dc * 512:(dc + 1) * 512],
                              apd[dc, :, tb * 512:(tb + 1) * 512].opt(), ws)
            for kc in range(NKH):
                lap = ctx.dma("sync", "L4", xnb4[:, kc * 512:(kc + 1) * 512],
                              xn_b[kc, :, tb * 512:(tb + 1) * 512].opt(), ())
            t_ld4 = lap
            sk_ps = [pC[0:16, 0:512], pC[0:16, 512:1024], y_ps[0:16, 0:512]]
            if tb == 0:
                wlora = [tlB, t_ld4, t_masks, cons0, cons1, t_yps_free]
            else:
                wlora = [t_ld4, sk_dv]
            for k in range(3):
                for kc in range(NKH):
                    pe(lambda e, k=k, kc=kc: e.matmul(
                        sk_ps[k],
                        lA4[:, (k * NKH + kc) * NST:(k * NKH + kc + 1) * NST],
                        xnb4[:, kc * 512:(kc + 1) * 512],
                        start=(kc == 0), stop=(kc == NKH - 1)),
                        wlora if (k == 0 and kc == 0) else (),
                        inc=(kc == NKH - 1))
            t_skmm = (PE, ctx.val(PE))
            for k in range(3):
                sk_dv = dve(lambda e, k=k: e.tensor_scalar(
                    out=skm4[:, k * 512:(k + 1) * 512], in0=sk_ps[k],
                    scalar1=s_masks[:, k:k + 1], scalar2=None, op0=ALU.mult),
                    [t_skmm, tlB] if k == 0 else ())
            for mc in range(NMO):
                sl = mc % 2
                g4 = tb * NMO + mc
                wld = [wg_pe.get(g4 - 2)]
                if g4 == 0:
                    wld += [t_yps_free] + t_st3_done
                ctx.dma("gpsimd", f"GW{sl}", wout4[sl][:], w_out6[mc].opt(),
                        wld)
                twf = ctx.dma("gpsimd", f"GW{sl}", wffn4[sl][:],
                              wffn6[mc].opt(), ())
                txr = ctx.dma("sync", f"XF{sl}", xf4[sl][:],
                              xf[mc, :, tb * 512:(tb + 1) * 512].opt(),
                              [ev4.get(g4 - 2)])
                ps = pA if sl == 0 else pB
                waits = [twf, sk_dv, ev4.get(g4 - 2)]
                for kci in range(NDC):
                    pe(lambda e, kci=kci, ps=ps, sl=sl: e.matmul(
                        ps[:, 0:512], wout4[sl][:, kci * 128:(kci + 1) * 128],
                        ap4[:, kci * 512:(kci + 1) * 512],
                        start=(kci == 0), stop=False),
                        waits if kci == 0 else (), inc=False)
                for kci in range(NKH):
                    pe(lambda e, kci=kci, ps=ps, sl=sl: e.matmul(
                        ps[:, 0:512], wffn4[sl][:, kci * 128:(kci + 1) * 128],
                        xnb4[:, kci * 512:(kci + 1) * 512],
                        start=False, stop=False), inc=False)
                for k in range(3):
                    pe(lambda e, k=k, mc=mc, ps=ps: e.matmul(
                        ps[:, 0:512],
                        lB4[:, k * H + mc * 128:k * H + (mc + 1) * 128],
                        skm4[:, k * 512:(k + 1) * 512],
                        start=False, stop=(k == 2)), inc=(k == 2))
                last_wmm = wg_pe[g4] = (PE, ctx.val(PE))
                dve(lambda e, ps=ps, mc=mc, sl=sl: e.tensor_scalar(
                    out=o14[sl][:], in0=ps[:, 0:512],
                    scalar1=s_ffnb[:, mc:mc + 1], scalar2=None, op0=ALU.add),
                    [last_wmm, txr, out_ss.get(g4 - 2)], inc=False)
                ev4[g4] = dve(lambda e, sl=sl: e.tensor_add(
                    out=ot4[sl][:], in0=o14[sl][:], in1=xf4[sl][:]))
                out_ss[g4] = ctx.dma(
                    "scalar", f"OT{sl}",
                    out[mc, :, tb * 512:(tb + 1) * 512].opt(),
                    ot4[sl][:], [ev4[g4]])
    t_fin0 = ctx.val("OT0")
    t_fin1 = ctx.val("OT1")

    def fin(e, v0=t_fin0, v1=t_fin1):
        e.wait_ge(ctx.sems["OT0"], v0)
        e.wait_ge(ctx.sems["OT1"], v1)
        return None
    ctx.add("sync", fin)
    ctx.add("gpsimd", fin)

    ctx.emit_all()
    return nc


def _prep_inputs(inputs):
    x = np.asarray(inputs["L0_attn_hidden_states"], np.float32)
    W_in = np.asarray(inputs["L0_attn_W_in"], np.float32)
    conv_w = np.asarray(inputs["L0_attn_conv_w"], np.float32)
    conv_b = np.asarray(inputs["L0_attn_conv_b"], np.float32)
    W_x = np.asarray(inputs["L0_attn_W_x"], np.float32)
    W_dt = np.asarray(inputs["L0_attn_W_dt"], np.float32)
    b_dt = np.asarray(inputs["L0_attn_b_dt"], np.float32)
    A_log = np.asarray(inputs["L0_attn_A_log"], np.float32)
    D = np.asarray(inputs["L0_attn_D"], np.float32)
    W_out = np.asarray(inputs["L0_attn_W_out"], np.float32)
    Wf = np.asarray(inputs["L0_ffn_weight"], np.float32)
    fb = np.asarray(inputs["L0_ffn_bias"], np.float32)
    lA = np.asarray(inputs["L0_ffn_lora_A"], np.float32)
    lB = np.asarray(inputs["L0_ffn_lora_B"], np.float32)
    lsi_v = np.asarray(inputs["L0_ffn_lora_start_indices"]).astype(np.int32)

    C = np.ascontiguousarray
    ind = np.zeros((NST, NST * 128), np.float32)
    for n in range(NST):
        ind[n, n * 128:(n + 1) * 128] = 1.0
    shared = {
        "w_in6": C(W_in.T.reshape(NKH, 128, NMP, 128).transpose(2, 1, 0, 3)),
        "w_x6": C(W_x.T.reshape(NDC, 128, 288)),
        "w_dt6": C(W_dt.T.reshape(2, 128, NDC, 128).transpose(2, 1, 0, 3)),
        "w_out6": C(W_out.T.reshape(NDC, 128, NMO, 128).transpose(2, 1, 0, 3)),
        "wffn6": C(Wf.T.reshape(NKH, 128, NMO, 128).transpose(2, 1, 0, 3)),
        "lora_at": C(lA.transpose(0, 2, 1).reshape(3, NKH, 128, NST)
                     .transpose(2, 0, 1, 3).reshape(128, 3 * NKH * NST)),
        "lora_bt": C(lB.transpose(2, 0, 1).reshape(16, 3 * H)),
        "alog_t": C(A_log.reshape(NDC, 128, NST).transpose(1, 0, 2)
                    .reshape(128, NDC * NST)),
        "d_t": C(D.reshape(NDC, 128).T),
        "convw_t": C(conv_w.reshape(NDC, 128, KCV).transpose(1, 0, 2)
                     .reshape(128, NDC * KCV)),
        "convb_t": C(conv_b.reshape(NDC, 128).T),
        "bdt_t": C(b_dt.reshape(NDC, 128).T),
        "ffnb_t": C(fb.reshape(NMO, 128).T),
        "lsi": C(np.tile(lsi_v.reshape(1, 4), (16, 1))),
        "ident": np.eye(128, dtype=np.float32),
        "ind16": ind,
        "onescol": np.ones((128, 1), np.float32),
        "consts": np.tile(np.array([[1e-5, 1e-6, 1.0, 0.0]], np.float32),
                          (128, 1)),
        "onesrow": np.ones((1, 128), np.float32),
    }
    in_maps = []
    for i in range(NB):
        xi = C(x[i].T.reshape(NKH, 128, T))
        m = dict(shared)
        m["xr"] = xi
        m["xf"] = xi
        m["cid"] = np.full((16, 1), float(i), np.float32)
        in_maps.append(m)
    return in_maps


def kernel(**inputs):
    in_maps = _prep_inputs(inputs)
    nc = build_nc()
    res = run_bass_kernel_spmd(nc, in_maps, core_ids=list(range(NB)))
    outs = []
    for r in res.results:
        outs.append(r["out"].reshape(H, T).T)
    return np.stack(outs, axis=0).astype(np.float32)
